# version 03
# baseline (speedup 1.0000x reference)
import math
import numpy as np
import concourse.bass as bass
import concourse.mybir as mybir
from concourse.bass_utils import run_bass_kernel_spmd

F32 = mybir.dt.float32
BF16 = mybir.dt.bfloat16
AF = mybir.ActivationFunctionType
ALU = mybir.AluOpType
AX = mybir.AxisListType

S = 2048
DM = 2048
NH, HS = 16, 128
NE = 16
IM = 1024
NT = 16
RSQ = 1.0 / math.sqrt(HS)

ENGS = ("pe", "act", "dve", "pool", "sp")
BLKNAME = {"pe": "tensor", "act": "scalar", "dve": "vector", "pool": "gpsimd", "sp": "sync"}
NDMASEM = 8


class P:
    def __init__(self, taps=(), phases=4):
        self.taps = set(taps)
        self.phases = phases
        self.nc = bass.Bass("TRN2", target_bir_lowering=False)
        self.ops = {e: [] for e in ENGS}
        self.res = {}
        self.cnt = {}
        self.waited = {e: {} for e in ENGS}
        self.sems = {}
        self.sem_order = []
        for e in ENGS:
            self._newsem("c_" + e)
        for q in ("sp", "pool"):
            for i in range(NDMASEM):
                self._newsem(f"d_{q}{i}")
        self.dma_i = {"sp": 0, "pool": 0}
        self.tap_tensors = {}

    def _newsem(self, name):
        self.cnt[name] = 0
        self.sem_order.append(name)

    def emit(self, eng, fn, reads=(), writes=(), dma=False, tag=""):
        waits = {}

        def addw(tok):
            if tok is not None:
                s, v = tok
                if v > waits.get(s, 0):
                    waits[s] = v

        for k in reads:
            r = self.res.get(k)
            if r:
                addw(r[0])
        for k in writes:
            r = self.res.get(k)
            if r:
                addw(r[0])
                for t in r[1]:
                    addw(t)
        w = self.waited[eng]
        waits = {s: v for s, v in waits.items() if v > w.get(s, 0)}
        for s, v in waits.items():
            w[s] = v
        if dma:
            i = self.dma_i[eng]
            self.dma_i[eng] += 1
            sem = f"d_{eng}{i % NDMASEM}"
            step = 16
            if self.cnt[sem] > self.waited[eng].get(sem, 0):
                waits[sem] = self.cnt[sem]
                self.waited[eng][sem] = self.cnt[sem]
        else:
            sem = "c_" + eng
            step = 1
        self.cnt[sem] += step
        tok = (sem, self.cnt[sem])
        for k in writes:
            self.res[k] = [tok, []]
        for k in reads:
            r = self.res.setdefault(k, [None, []])
            r[1].append(tok)
        self.ops[eng].append((waits, fn, sem, step, tag))
        return tok

    def inherit(self, newkeys, oldkeys):
        toks = []
        for k in oldkeys:
            r = self.res.get(k)
            if r:
                if r[0]:
                    toks.append(r[0])
                toks.extend(r[1])
        for k in newkeys:
            r = self.res.setdefault(k, [None, []])
            r[1].extend(toks)

    def finish(self):
        for q in ("sp", "pool"):
            waits = {}
            for i in range(NDMASEM):
                s = f"d_{q}{i}"
                if self.cnt[s] > self.waited[q].get(s, 0):
                    waits[s] = self.cnt[s]
            if waits:
                self.ops[q].append((waits, None, None, 0, "final"))

    def realize(self):
        nc = self.nc
        from contextlib import ExitStack
        stack = ExitStack()
        for name in self.sem_order:
            self.sems[name] = stack.enter_context(nc.semaphore(name))
        blk = stack.enter_context(nc.Block())
        for eng in ENGS:
            ops = self.ops[eng]
            if not ops:
                continue

            def runner(handle, ops=ops):
                for waits, fn, sem, step, tag in ops:
                    for s, v in waits.items():
                        handle.wait_ge(self.sems[s], v)
                    if fn is not None:
                        fn(handle).then_inc(self.sems[sem], step)
            getattr(blk, BLKNAME[eng])(runner)
        stack.close()

    def tap(self, name, shape, dtype):
        if name in self.taps and name not in self.tap_tensors:
            self.tap_tensors[name] = self.nc.dram_tensor(
                "tap_" + name, shape, dtype, kind="ExternalOutput").ap()
        return self.tap_tensors.get(name)

    def _enter(self, cm):
        from contextlib import ExitStack
        if not hasattr(self, "_alloc_stack"):
            self._alloc_stack = ExitStack()
        return self._alloc_stack.enter_context(cm)

    def wk(self, off, shape, dt):
        n_el = 1
        for d in shape[1:]:
            n_el *= d
        nbytes = n_el * (4 if dt == F32 else 2)
        a = self.wk_sb.ap()[:, off // 2:(off + nbytes) // 2]
        if dt == F32:
            a = a.bitcast(F32)
        if len(shape) == 3:
            a = a.rearrange("p (a b) -> p a b", b=shape[2])
        elif len(shape) == 4:
            a = a.rearrange("p (a b c) -> p a b c", b=shape[2], c=shape[3])
        return a

    def build(self):
        nc = self.nc
        self.x_d = nc.dram_tensor("x", [S, DM], F32, kind="ExternalInput").ap()
        self.wqkv_d = nc.dram_tensor("Wqkv", [DM, 3 * DM], F32, kind="ExternalInput").ap()
        self.bqkv_d = nc.dram_tensor("bqkv", [3 * DM], F32, kind="ExternalInput").ap()
        self.wo_d = nc.dram_tensor("Wo", [DM, DM], F32, kind="ExternalInput").ap()
        self.bo_d = nc.dram_tensor("bo", [DM], F32, kind="ExternalInput").ap()
        self.rw_d = nc.dram_tensor("rw", [NE, DM], F32, kind="ExternalInput").ap()
        self.ebias_d = nc.dram_tensor("e_bias", [NE], F32, kind="ExternalInput").ap()
        self.wup_d = nc.dram_tensor("w_up", [NE, DM, IM], F32, kind="ExternalInput").ap()
        self.wdn_d = nc.dram_tensor("w_down", [NE, IM, DM], F32, kind="ExternalInput").ap()
        self.shup_d = nc.dram_tensor("sh_up", [DM, 4 * IM], F32, kind="ExternalInput").ap()
        self.shdn_d = nc.dram_tensor("sh_down", [4 * IM, DM], F32, kind="ExternalInput").ap()
        self.out_d = nc.dram_tensor("out", [S, DM], F32, kind="ExternalOutput").ap()
        self.combT_d = nc.dram_tensor("combT_scr", [NE, S], F32, kind="Internal").ap()
        self.g_d = nc.dram_tensor("g_scr", [S, DM], BF16, kind="Internal").ap()

        def sb(n, shp, dt):
            setattr(self, n, self._enter(nc.sbuf_tensor(n, shp, dt)))
        sb("xn_sb", [128, NT, DM], BF16)
        sb("big_sb", [128, NT, DM], BF16)
        sb("wk_sb", [128, 36864], BF16)
        sb("ident_f", [128, 128], F32)
        sb("ident_b", [128, 128], BF16)
        sb("mask_add", [128, 128], F32)
        sb("rwT", [128, NT, NE], F32)
        sb("bq_cols", [128, 48], F32)
        sb("bq_sc", [128, NH], F32)
        sb("ebias_bc", [128, NE], F32)
        sb("eidx", [128, NE], F32)
        sb("gidx", [128, 4], F32)
        sb("combine", [128, NT, NE], F32)
        sb("logits", [128, NT, NE], F32)
        sb("cstage", [16, 128], F32)
        sb("bstage", [48, 128], F32)
        sb("rst", [16, 2, 128], F32)
        sb("stats", [128, 32], F32)
        sb("rsc", [128, 8, 16], F32)
        self.psA = self._enter(nc.psum_tensor("psA", [128, 512], F32))
        self.psB = self._enter(nc.psum_tensor("psB", [128, 512], F32))
        self.psC = self._enter(nc.psum_tensor("psC", [128, 2048], F32))
        self.psD = self._enter(nc.psum_tensor("psD", [128, 1024], BF16))
        self.psE = self._enter(nc.psum_tensor("psE", [128, 128], F32))

        self.setup()
        self.p0()
        if self.phases >= 2:
            self.p1()
        if self.phases >= 3:
            self.p2()
        if self.phases >= 4:
            self.p3()
        self.finish()
        self.realize()
        return nc

    def setup(self):
        E = self.emit
        E("pool", lambda g: g.memset(self.ident_f[:], 0.0), writes=["idf"])
        E("pool", lambda g: g.affine_select(out=self.ident_f[:], in_=self.ident_f[:],
          compare_op=ALU.not_equal, fill=1.0, base=0, pattern=[[-1, 128]],
          channel_multiplier=1), reads=["idf"], writes=["idf"])
        E("pool", lambda g: g.memset(self.ident_b[:], 0.0), writes=["idb"])
        E("pool", lambda g: g.affine_select(out=self.ident_b[:], in_=self.ident_b[:],
          compare_op=ALU.not_equal, fill=1.0, base=0, pattern=[[-1, 128]],
          channel_multiplier=1), reads=["idb"], writes=["idb"])
        E("pool", lambda g: g.memset(self.mask_add[:], 0.0), writes=["mask"])
        E("pool", lambda g: g.affine_select(out=self.mask_add[:], in_=self.mask_add[:],
          compare_op=ALU.is_ge, fill=-30000.0, base=0, pattern=[[-1, 128]],
          channel_multiplier=1), reads=["mask"], writes=["mask"])
        E("pool", lambda g: g.iota(self.eidx[:], pattern=[[1, NE]], base=0,
          channel_multiplier=0, allow_small_or_imprecise_dtypes=True), writes=["eidx"])
        E("pool", lambda g: g.iota(self.gidx[:], pattern=[[1, 4]], base=0,
          channel_multiplier=0, allow_small_or_imprecise_dtypes=True), writes=["gidx"])
        E("sp", lambda s: s.dma_start(self.ebias_bc[:],
          self.ebias_d.rearrange("(o n) -> o n", o=1).partition_broadcast(128)),
          writes=["ebias"], dma=True)
        E("sp", lambda s: s.dma_start(self.bstage[:],
          self.bqkv_d.rearrange("(a b) -> a b", b=128)), writes=["bstage"], dma=True)
        E("pe", lambda t: t.transpose(self.psA[:, 0:48], self.bstage[:],
          self.ident_f[0:48, 0:48]), reads=["bstage", "idf"], writes=["psA"])
        E("dve", lambda v: v.tensor_copy(self.bq_cols[:], self.psA[:, 0:48]),
          reads=["psA"], writes=["bqc"])
        E("dve", lambda v: v.tensor_scalar(self.bq_sc[:], self.bq_cols[:, 0:16],
          RSQ, None, op0=ALU.mult), reads=["bqc"], writes=["bqsc"])
        for ht in range(NT):
            j = ht % 2
            E("sp", lambda s, ht=ht, j=j: s.dma_start(self.rst[:, j, :],
              self.rw_d[:, ht * 128:(ht + 1) * 128]), writes=[f"rst{j}"], dma=True)
            E("pe", lambda t, j=j: t.transpose(self.psB[:, 0:16],
              self.rst[:, j, :], self.ident_f[0:16, 0:16]),
              reads=[f"rst{j}", "idf"], writes=["psB"])
            E("dve", lambda v, ht=ht: v.tensor_copy(self.rwT.ap()[:, ht, :],
              self.psB[:, 0:16]), reads=["psB"], writes=["rwT"])

    def p0(self):
        import os
        sub = int(os.environ.get("P0SUB", "9"))
        E = self.emit
        xbuf = [self.wk(0, [128, DM], F32), self.wk(8192, [128, DM], F32)]
        xnb = [self.wk(16384, [128, DM], F32), self.wk(24576, [128, DM], F32)]
        dump = self.wk(32768, [128, DM], F32)
        dump2 = self.wk(43008, [128, DM], F32)
        xt32 = [self.wk(40960, [128, 256], F32), self.wk(41984, [128, 256], F32)]
        st = self.stats.ap()
        tapxn = self.tap("xn", [S, DM], F32)

        for tp in range(8):
            for j in range(2):
                tt = tp * 2 + j
                xb, xn = xbuf[j], xnb[j]
                kx, kxn = f"xb{j}", f"xn{j}"
                E("sp", lambda s, tt=tt, xb=xb: s.dma_start(
                    xb, self.x_d[tt * 128:(tt + 1) * 128, :]),
                    writes=[kx], dma=True, tag=f"ldx{tt}")
                o = j * 5
                dmp = [dump, dump2][j]
                kd = f"dump{j}"
                E("dve", lambda v, xb=xb, o=o: v.reduce_sum(st[:, o:o + 1], xb,
                  axis=AX.X, negate=True), reads=[kx], writes=[f"st{o}"])
                E("act", lambda a, xb=xb, o=o, dmp=dmp: a.activation(dmp, xb,
                  AF.Square, accum_out=st[:, o + 1:o + 2]),
                  reads=[kx], writes=[kd, f"st{o+1}"])
                E("dve", lambda v, o=o: v.tensor_scalar(st[:, o:o + 1], st[:, o:o + 1],
                  1.0 / DM, None, op0=ALU.mult), reads=[f"st{o}"], writes=[f"st{o}"])
                E("dve", lambda v, o=o: v.tensor_scalar(st[:, o + 1:o + 2], st[:, o + 1:o + 2],
                  1.0 / DM, None, op0=ALU.mult), reads=[f"st{o+1}"], writes=[f"st{o+1}"])
                E("dve", lambda v, o=o: v.tensor_tensor(st[:, o + 2:o + 3], st[:, o:o + 1],
                  st[:, o:o + 1], op=ALU.mult), reads=[f"st{o}"], writes=[f"st{o+2}"])
                E("dve", lambda v, o=o: v.tensor_tensor(st[:, o + 2:o + 3], st[:, o + 1:o + 2],
                  st[:, o + 2:o + 3], op=ALU.subtract), reads=[f"st{o+1}", f"st{o+2}"], writes=[f"st{o+2}"])
                E("dve", lambda v, o=o: v.tensor_scalar(st[:, o + 2:o + 3], st[:, o + 2:o + 3],
                  1e-5, None, op0=ALU.add), reads=[f"st{o+2}"], writes=[f"st{o+2}"])
                E("act", lambda a, o=o: a.activation(st[:, o + 3:o + 4], st[:, o + 2:o + 3], AF.Ln),
                  reads=[f"st{o+2}"], writes=[f"st{o+3}"])
                E("act", lambda a, o=o: a.activation(st[:, o + 3:o + 4], st[:, o + 3:o + 4], AF.Exp,
                  scale=-0.5), reads=[f"st{o+3}"], writes=[f"st{o+3}"])
                E("dve", lambda v, o=o: v.tensor_tensor(st[:, o + 4:o + 5], st[:, o:o + 1],
                  st[:, o + 3:o + 4], op=ALU.mult), reads=[f"st{o}", f"st{o+3}"], writes=[f"st{o+4}"])
                E("act", lambda a, xb=xb, xn=xn, o=o: a.activation(xn, xb, AF.Identity,
                  bias=st[:, o + 4:o + 5], scale=st[:, o + 3:o + 4]),
                  reads=[kx, f"st{o+3}", f"st{o+4}"], writes=[kxn])
                if tapxn is not None:
                    E("sp", lambda s, tt=tt, xn=xn: s.dma_start(
                        tapxn[tt * 128:(tt + 1) * 128, :], xn), reads=[kxn], dma=True)
            if sub < 2:
                continue
            for ht in range(NT):
                bank = ht % 4
                pc = self.psC[:, bank * 512:bank * 512 + 512]
                kb = f"psC{bank}"
                E("pe", lambda t, ht=ht, pc=pc: t.transpose(
                    pc[:, 0:128], xnb[0][:, ht * 128:(ht + 1) * 128], self.ident_f[:]),
                    reads=["xn0", "idf"], writes=[kb])
                E("pe", lambda t, ht=ht, pc=pc: t.transpose(
                    pc[:, 128:256], xnb[1][:, ht * 128:(ht + 1) * 128], self.ident_f[:]),
                    reads=["xn1", "idf"], writes=[kb])
                E("dve", lambda v, ht=ht, tp=tp, pc=pc: v.tensor_copy(
                    self.xn_sb.ap()[:, ht, tp * 256:tp * 256 + 256], pc[:, 0:256]),
                    reads=[kb], writes=[f"xn:{ht}:{tp // 4}"])
                j2 = ht % 2
                if sub >= 3:
                    E("act", lambda a, pc=pc, j2=j2: a.activation(xt32[j2], pc[:, 0:256], AF.Identity),
                        reads=[kb, f"xn:{ht}:{tp // 4}"], writes=[f"xt32{j2}"])
                if sub >= 4:
                    E("pe", lambda t, ht=ht, j2=j2: t.matmul(
                        self.psA[0:16, 0:256], self.rwT.ap()[:, ht, :], xt32[j2],
                        start=(ht == 0), stop=(ht == NT - 1)),
                        reads=[f"xt32{j2}", "rwT"], writes=["psA"])
            if sub < 5:
                continue
            for j in range(2):
                E("dve", lambda v, j=j: v.tensor_copy(self.cstage[:, 0:128],
                  self.psA[0:16, j * 128:j * 128 + 128]),
                  reads=["psA"], writes=["cst"])
                E("pe", lambda t: t.transpose(self.psB[:, 0:16],
                  self.cstage[:, 0:128], self.ident_f[0:16, 0:16]),
                  reads=["cst", "idf"], writes=["psB"])
                E("dve", lambda v, tp=tp, j=j: v.tensor_copy(
                  self.logits.ap()[:, tp * 2 + j, :], self.psB[:, 0:16]),
                  reads=["psB"], writes=["logits"])
        if sub < 6:
            return
        taplog = self.tap("logits", [S, NE], F32)
        tapcmb = self.tap("combine", [S, NE], F32)
        r = [self.rsc.ap()[:, i, :] for i in range(8)]

        def D(fn, reads=(), writes=()):
            E("dve", fn, reads=list(reads) + ["dch"], writes=list(writes) + ["dch"])

        for tt in range(NT):
            L = self.logits.ap()[:, tt, :]
            if taplog is not None:
                E("sp", lambda s, tt=tt, L=L: s.dma_start(
                    taplog[tt * 128:(tt + 1) * 128, :], L), reads=["logits"], dma=True)
            E("act", lambda a, L=L: a.activation(r[0], L, AF.Sigmoid),
              reads=["logits", "dch"], writes=["r0"])
            D(lambda v: v.tensor_tensor(r[1], r[0], self.ebias_bc[:],
              op=ALU.add), reads=["r0", "ebias"])
            sbg = r[1].rearrange("p (g e) -> p g e", e=4)
            pairs = [(0, 1), (0, 2), (0, 3), (1, 2), (1, 3), (2, 3)]
            for pi, (i, j) in enumerate(pairs):
                dst = self.rsc.ap()[:, 2 + pi // 4, (pi % 4) * 4:(pi % 4) * 4 + 4]
                D(lambda v, i=i, j=j, dst=dst, sbg=sbg: v.tensor_tensor(
                  dst, sbg[:, :, i], sbg[:, :, j], op=ALU.add))
            gs = self.rsc.ap()[:, 4, 0:4]
            D(lambda v: v.tensor_tensor(gs, self.rsc.ap()[:, 2, 0:4],
              self.rsc.ap()[:, 2, 4:8], op=ALU.max))
            for pi in range(2, 6):
                src = self.rsc.ap()[:, 2 + pi // 4, (pi % 4) * 4:(pi % 4) * 4 + 4]
                D(lambda v, src=src: v.tensor_tensor(gs, gs, src, op=ALU.max))
            st = self.stats.ap()

            def idx_pick(vec, idxtile, kidx, mcol, icol, nfree):
                D(lambda v: v.tensor_reduce(st[:, mcol:mcol + 1], vec,
                  axis=AX.X, op=ALU.max))
                c1 = self.rsc.ap()[:, 5, 0:nfree]
                c2 = self.rsc.ap()[:, 6, 0:nfree]
                D(lambda v: v.tensor_scalar(c1, vec, st[:, mcol:mcol + 1],
                  None, op0=ALU.is_ge))
                D(lambda v: v.tensor_scalar(c2, idxtile, -99.0, None, op0=ALU.add),
                  reads=[kidx])
                D(lambda v: v.tensor_tensor(c2, c1, c2, op=ALU.mult))
                D(lambda v: v.tensor_scalar(c2, c2, 99.0, None, op0=ALU.add))
                D(lambda v: v.tensor_reduce(st[:, icol:icol + 1], c2,
                  axis=AX.X, op=ALU.min))

            idx_pick(gs, self.gidx[:], "gidx", 8, 9, 4)
            gm1 = self.rsc.ap()[:, 7, 0:4]
            D(lambda v: v.tensor_scalar(gm1, self.gidx[:], st[:, 9:10], None,
              op0=ALU.is_equal), reads=["gidx"])
            gs2 = self.rsc.ap()[:, 7, 8:12]
            D(lambda v: v.tensor_scalar(gs2, gm1, -1e30, None, op0=ALU.mult))
            D(lambda v: v.tensor_tensor(gs2, gs, gs2, op=ALU.add))
            idx_pick(gs2, self.gidx[:], "gidx", 10, 11, 4)
            gm2 = self.rsc.ap()[:, 7, 12:16]
            D(lambda v: v.tensor_scalar(gm2, self.gidx[:], st[:, 11:12], None,
              op0=ALU.is_equal), reads=["gidx"])
            gmask = self.rsc.ap()[:, 4, 4:8]
            D(lambda v: v.tensor_tensor(gmask, gm1, gm2, op=ALU.add))
            masked = r[2]
            sm = masked.rearrange("p (g e) -> p g e", e=4)
            for jj in range(4):
                D(lambda v, jj=jj: v.tensor_copy(sm[:, :, jj], gmask))
            D(lambda v: v.tensor_tensor(masked, r[1], masked, op=ALU.mult))
            idx_pick(masked, self.eidx[:], "eidx", 12, 13, 16)
            rm1 = r[3]
            D(lambda v: v.tensor_scalar(rm1, self.eidx[:], st[:, 13:14], None,
              op0=ALU.is_equal), reads=["eidx"])
            m2t = self.rsc.ap()[:, 4, 0:16]
            D(lambda v: v.tensor_scalar(m2t, rm1, -1e30, None, op0=ALU.mult))
            D(lambda v: v.tensor_tensor(m2t, masked, m2t, op=ALU.add))
            idx_pick(m2t, self.eidx[:], "eidx", 14, 15, 16)
            rm2 = self.rsc.ap()[:, 5, 0:16]
            D(lambda v: v.tensor_scalar(rm2, self.eidx[:], st[:, 15:16], None,
              op0=ALU.is_equal), reads=["eidx"])
            wtmp = self.rsc.ap()[:, 6, 0:16]
            D(lambda v: v.tensor_tensor(wtmp, r[0], rm1, op=ALU.mult))
            D(lambda v: v.reduce_sum(st[:, 16:17], wtmp, axis=AX.X))
            D(lambda v: v.tensor_tensor(wtmp, r[0], rm2, op=ALU.mult))
            D(lambda v: v.reduce_sum(st[:, 17:18], wtmp, axis=AX.X))
            D(lambda v: v.tensor_tensor(st[:, 18:19], st[:, 16:17],
              st[:, 17:18], op=ALU.add))
            D(lambda v: v.tensor_scalar(st[:, 18:19], st[:, 18:19], 1e-20,
              None, op0=ALU.add))
            D(lambda v: v.reciprocal(st[:, 19:20], st[:, 18:19]))
            D(lambda v: v.tensor_tensor(st[:, 16:17], st[:, 16:17],
              st[:, 19:20], op=ALU.mult))
            D(lambda v: v.tensor_tensor(st[:, 17:18], st[:, 17:18],
              st[:, 19:20], op=ALU.mult))
            C = self.combine.ap()[:, tt, :]
            D(lambda v, C=C: v.tensor_scalar(C, rm1, st[:, 16:17], None,
              op0=ALU.mult))
            D(lambda v: v.tensor_scalar(wtmp, rm2, st[:, 17:18], None,
              op0=ALU.mult))
            D(lambda v, C=C: v.tensor_tensor(C, C, wtmp, op=ALU.add),
              writes=["comb"])
            if tapcmb is not None:
                E("sp", lambda s, tt=tt, C=C: s.dma_start(
                    tapcmb[tt * 128:(tt + 1) * 128, :], C), reads=["comb"], dma=True)
            E("pe", lambda t, C=C: t.transpose(self.psB[0:16, 0:128], C,
              self.ident_f[:]), reads=["comb", "dch", "idf"], writes=["psB"])
            E("dve", lambda v: v.tensor_copy(self.cstage[:], self.psB[0:16, 0:128]),
              reads=["psB"], writes=["cst"])
            E("sp", lambda s, tt=tt: s.dma_start(
              self.combT_d[:, tt * 128:(tt + 1) * 128], self.cstage[:]),
              reads=["cst"], writes=[f"cT{tt}"], dma=True)

    def p1(self):
        import os
        tape = int(os.environ.get("P1TAPE", "-1"))
        E = self.emit
        self.inherit(["Aebuf", "wdn0", "wdn1", "wup0a", "wup0b", "wup1a", "wup1b",
                      "Cb", "scr1a", "scr1b", "scr2", "gst"],
                     ["xb0", "xb1", "xn0", "xn1", "dump0", "dump1", "xt320", "xt321"])
        Ae = self.wk(0, [128, 8, IM], BF16)
        wdn = self.wk(16384, [128, 8, DM], BF16)
        wup = [self.wk(49152, [128, NT, 128], BF16),
               self.wk(53248, [128, NT, 128], BF16)]
        Cb = self.wk(57344, [128, 1024], F32)
        scr1 = [self.wk(61440, [128, 512], F32), self.wk(63488, [128, 512], F32)]
        scr2 = self.wk(65536, [128, 512], F32)
        gst = self.wk(67584, [128, DM], BF16)
        routed = self.big_sb.ap().rearrange("p a b -> p (a b)").bitcast(F32) \
                     .rearrange("p (tt h) -> p tt h", h=DM)
        taprt = self.tap("routed", [S, DM], F32)
        tapg = self.tap("g", [S, DM], BF16)

        for H in range(2):
            t0 = H * 1024
            import os as _os
            if _os.environ.get("SHFIRST", "1") == "1":
                order = [16, 17, 18, 19] + list(range(NE))
            else:
                order = list(range(20))
            for e in order:
                first = e == order[0]
                moe = e < NE
                def load_up(ic):
                    wb = wup[ic % 2]
                    if moe:
                        src = self.wup_d[e, :, ic * 128:(ic + 1) * 128]
                    else:
                        jj = e - NE
                        src = self.shup_d[:, jj * IM + ic * 128:jj * IM + (ic + 1) * 128]
                    for half in range(2):
                        E("pool", lambda g, src=src, wb=wb, half=half: g.dma_start(
                          wb[:, half * 8:half * 8 + 8, :],
                          src.rearrange("(ht p) c -> p ht c", p=128)[:, half * 8:half * 8 + 8, :]),
                          writes=[f"wup{ic % 2}{'ab'[half]}"], dma=True)
                load_up(0)
                load_up(1)
                for half in range(2):
                    if moe:
                        src2 = self.wdn_d[e, half * 512:half * 512 + 512, :]
                    else:
                        jj = e - NE
                        src2 = self.shdn_d[jj * IM + half * 512:jj * IM + half * 512 + 512, :]
                    E("pool", lambda g, src2=src2, half=half: g.dma_start(
                      wdn[:, half * 4:half * 4 + 4, :],
                      src2.rearrange("(ii p) h -> p ii h", p=128)),
                      writes=[f"wdn{half}"], dma=True)
                if moe:
                    E("sp", lambda s, e=e, t0=t0: s.dma_start(Cb,
                      self.combT_d[e:e + 1, t0:t0 + 1024].partition_broadcast(128)),
                      reads=[f"cT{tt}" for tt in range(H * 8, H * 8 + 8)],
                      writes=["Cb"], dma=True)
                for ic in range(8):
                    if ic >= 2:
                        pass
                    wb = wup[ic % 2]
                    kwb = [f"wup{ic % 2}a", f"wup{ic % 2}b"]
                    for tc in range(2):
                        ps = [self.psA, self.psB][tc]
                        kps = ["psA", "psB"][tc]

                        def mmup(t, wb=wb, ps=ps, t0=t0, tc=tc):
                            for ht in range(NT):
                                mm = t.matmul(ps[:], wb[:, ht, :],
                                    self.xn_sb.ap()[:, ht, t0 + tc * 512:t0 + tc * 512 + 512],
                                    start=(ht == 0), stop=(ht == NT - 1))
                            return mm
                        E("pe", mmup, reads=kwb + [f"xn:{h}:{H}" for h in range(NT)],
                          writes=[kps], tag=f"up{e}")
                        s1 = scr1[tc]
                        ks1 = f"scr1{'ab'[tc]}"
                        E("act", lambda a, ps=ps, s1=s1: a.activation(s1, ps[:],
                          AF.Relu), reads=[kps], writes=[ks1])
                        dst = Ae[:, ic, tc * 512:tc * 512 + 512]
                        if moe:
                            E("dve", lambda v, s1=s1, tc=tc: v.tensor_tensor(
                              scr2, s1, Cb[:, tc * 512:tc * 512 + 512], op=ALU.mult),
                              reads=[ks1, "Cb"], writes=["scr2"])
                            E("dve", lambda v, s1=s1, dst=dst: v.tensor_tensor(
                              dst, scr2, s1, op=ALU.mult),
                              reads=["scr2", ks1], writes=["Aebuf"])
                        else:
                            E("act", lambda a, s1=s1, dst=dst: a.activation(
                              dst, s1, AF.Square), reads=[ks1], writes=["Aebuf"])
                    if ic + 2 < 8:
                        load_up(ic + 2)
                if H == 0 and e == tape:
                    tA = self.tap("Ae", [8 * 128, IM], BF16)
                    tC = self.tap("Cb", [128, 1024], F32)
                    if tA is not None:
                        for ii in range(8):
                            E("sp", lambda s, ii=ii, tA=tA: s.dma_start(
                              tA[ii * 128:(ii + 1) * 128, :], Ae[:, ii, :]),
                              reads=["Aebuf"], dma=True)
                        if e < NE:
                            E("sp", lambda s, tC=tC: s.dma_start(tC, Cb),
                              reads=["Cb"], dma=True)
                for tt in range(8):
                    for hh in range(2):
                        pc = self.psC[:, hh * 1024:hh * 1024 + 1024]
                        kbs = [f"psC{hh * 2}", f"psC{hh * 2 + 1}"]

                        def mmdn(t, tt=tt, hh=hh, pc=pc):
                            for hc in range(2):
                                for ii in range(8):
                                    mm = t.matmul(
                                        pc[:, hc * 512:hc * 512 + 512],
                                        Ae[:, ii, tt * 128:tt * 128 + 128],
                                        wdn[:, ii, hh * 1024 + hc * 512:hh * 1024 + hc * 512 + 512],
                                        start=(ii == 0), stop=(ii == 7))
                            return mm
                        E("pe", mmdn, reads=["Aebuf", "wdn0", "wdn1"],
                          writes=kbs, tag=f"dn{e}")
                        dst = routed[:, tt, hh * 1024:hh * 1024 + 1024]
                        if first:
                            E("dve", lambda v, dst=dst, pc=pc: v.tensor_copy(
                              dst, pc), reads=kbs, writes=[f"rt{tt}{hh}"])
                        else:
                            E("dve", lambda v, dst=dst, pc=pc: v.tensor_tensor(
                              dst, dst, pc, op=ALU.add),
                              reads=kbs + [f"rt{tt}{hh}"], writes=[f"rt{tt}{hh}"])
            for tt in range(8):
                gt = H * 8 + tt
                E("act", lambda a, tt=tt: a.activation(gst, routed[:, tt, :],
                  AF.Silu), reads=[f"rt{tt}0", f"rt{tt}1", "gst"], writes=["gst"])
                E("sp", lambda s, gt=gt: s.dma_start(
                  self.g_d[gt * 128:(gt + 1) * 128, :], gst),
                  reads=["gst"], writes=[f"g:{gt}"], dma=True)
                if taprt is not None:
                    E("sp", lambda s, gt=gt, tt=tt: s.dma_start(
                      taprt[gt * 128:(gt + 1) * 128, :], routed[:, tt, :]),
                      reads=[f"rt{tt}0", f"rt{tt}1"], dma=True)
                if tapg is not None:
                    E("sp", lambda s, gt=gt: s.dma_start(
                      tapg[gt * 128:(gt + 1) * 128, :], gst), reads=["gst"], dma=True)

    def p2(self):
        E = self.emit
        self.inherit(["qsb0", "qsb1", "ksb0", "ksb1", "vsb0", "vsb1",
                      "expS", "Asb", "atb"]
                     + [f"sl{j}_{c}{h}" for j in range(2) for c in range(3) for h in range(2)],
                     ["Aebuf", "wdn0", "wdn1", "wup0a", "wup0b", "wup1a", "wup1b",
                      "Cb", "scr1a", "scr1b", "scr2", "gst"])
        q_sb = [self.wk(0, [128, S], BF16), self.wk(4096, [128, S], BF16)]
        k_sb = [self.wk(8192, [128, S], BF16), self.wk(12288, [128, S], BF16)]
        v_sb = [self.wk(16384, [128, NT, HS], BF16), self.wk(20480, [128, NT, HS], BF16)]
        slab = [self.wk(24576, [128, 3, NT, 128], BF16),
                self.wk(36864, [128, 3, NT, 128], BF16)]
        expS = self.wk(49152, [128, S], F32)
        A_sb = self.wk(57344, [128, S], BF16)
        at_sb = self.wk(61440, [128, S], BF16)
        st = self.stats.ap()
        tapO = self.tap("O", [S, DM], BF16)
        xnk = [f"xn:{h}:0" for h in range(NT)] + [f"xn:{h}:1" for h in range(NT)]

        def load_slab(n):
            jb = n % 2
            for c in range(3):
                for half in range(2):
                    E("pool", lambda g, n=n, c=c, half=half, jb=jb: g.dma_start(
                      slab[jb][:, c, half * 8:half * 8 + 8, :],
                      self.wqkv_d[:, c * DM + n * 128:c * DM + (n + 1) * 128]
                      .rearrange("(ht p) c2 -> p ht c2", p=128)[:, half * 8:half * 8 + 8, :]),
                      writes=[f"sl{jb}_{c}{half}"], dma=True, tag=f"slab{n}")

        def qkv_steps(n):
            jb = n % 2
            kslab = [f"sl{jb}_{c}{h}" for c in range(3) for h in range(2)]
            steps = []
            for c, dsti, kdst, bias, scale in (
                    (0, 0, f"qsb{jb}", self.bq_sc[:, n:n + 1], RSQ),
                    (1, 0, f"ksb{jb}", self.bq_cols[:, 16 + n:17 + n], 1.0)):
                dst = (q_sb if c == 0 else k_sb)[jb]
                for tc in range(4):
                    ps = [self.psA, self.psB][tc % 2]
                    kps = ["psA", "psB"][tc % 2]

                    def step(jb=jb, c=c, ps=ps, kps=kps, tc=tc, dst=dst,
                             kdst=kdst, bias=bias, scale=scale, kslab=kslab):
                        def mmqk(t):
                            for ht in range(NT):
                                mm = t.matmul(ps[:], slab[jb][:, c, ht, :],
                                    self.xn_sb.ap()[:, ht, tc * 512:tc * 512 + 512],
                                    start=(ht == 0), stop=(ht == NT - 1))
                            return mm
                        E("pe", mmqk, reads=kslab + xnk, writes=[kps], tag=f"qk{jb}")
                        E("act", lambda a: a.activation(
                          dst[:, tc * 512:tc * 512 + 512], ps[:], AF.Identity,
                          bias=bias, scale=scale),
                          reads=[kps, "bqc", "bqsc"], writes=[kdst])
                    steps.append(step)
            for tt in range(NT):
                psv = [self.psA, self.psB][tt % 2]
                kpsv = ["psA", "psB"][tt % 2]

                def step(jb=jb, tt=tt, psv=psv, kpsv=kpsv, kslab=kslab):
                    def mmv(t):
                        for ht in range(NT):
                            mm = t.matmul(psv[:, 0:128],
                                self.xn_sb.ap()[:, ht, tt * 128:tt * 128 + 128],
                                slab[jb][:, 2, ht, :], start=(ht == 0), stop=(ht == NT - 1))
                        return mm
                    E("pe", mmv, reads=kslab + xnk, writes=[kpsv], tag=f"v{jb}")
                    E("dve", lambda v: v.tensor_copy(v_sb[jb][:, tt, :], psv[:, 0:128]),
                      reads=[kpsv], writes=[f"vsb{jb}"])
                steps.append(step)
            return steps

        def attn_steps(n):
            jb = n % 2
            kq, kk, kv = f"qsb{jb}", f"ksb{jb}", f"vsb{jb}"
            steps = []
            for tt in range(NT):
                def step(n=n, jb=jb, tt=tt, kq=kq, kk=kk, kv=kv):
                    ln = (tt + 1) * 128
                    nch = (ln + 511) // 512
                    for ch in range(nch):
                        w = min(512, ln - ch * 512)
                        pc = self.psC[:, ch * 512:ch * 512 + w]
                        kb = f"psC{ch}"
                        E("pe", lambda t, tt=tt, w=w, ch=ch, pc=pc: t.matmul(
                          pc, q_sb[jb][:, tt * 128:tt * 128 + 128],
                          k_sb[jb][:, ch * 512:ch * 512 + w], start=True, stop=True),
                          reads=[kq, kk], writes=[kb], tag=f"s{n}")
                        if ch == tt // 4:
                            off = (tt % 4) * 128
                            E("dve", lambda v, pc=pc, off=off: v.tensor_tensor(
                              pc[:, off:off + 128], pc[:, off:off + 128],
                              self.mask_add[:], op=ALU.add),
                              reads=[kb, "mask"], writes=[kb])
                        E("act", lambda a, pc=pc, ch=ch, w=w: a.activation(
                          expS[:, ch * 512:ch * 512 + w], pc, AF.Exp,
                          accum_out=st[:, 25 + ch:26 + ch]),
                          reads=[kb], writes=["expS", f"st{25 + ch}"])
                    E("dve", lambda v, nch=nch: v.tensor_reduce(st[:, 29:30],
                      st[:, 25:25 + nch], axis=AX.X, op=ALU.add),
                      reads=[f"st{25 + ch2}" for ch2 in range(nch)], writes=["st29"])
                    E("dve", lambda v: v.reciprocal(st[:, 30:31], st[:, 29:30]),
                      reads=["st29"], writes=["st30"])
                    E("dve", lambda v, ln=ln: v.tensor_scalar(A_sb[:, 0:ln],
                      expS[:, 0:ln], st[:, 30:31], None, op0=ALU.mult),
                      reads=["expS", "st30"], writes=["Asb"])
                    for r0 in range(0, tt + 1, 8):
                        nb = min(8, tt + 1 - r0)

                        def mmtp(t, r0=r0, nb=nb):
                            for b in range(nb):
                                mm = t.transpose(self.psD[:, b * 128:b * 128 + 128],
                                    A_sb[:, (r0 + b) * 128:(r0 + b) * 128 + 128],
                                    self.ident_b[:])
                            return mm
                        E("pe", mmtp, reads=["Asb", "idb"], writes=["psD"], tag=f"tp{n}")
                        E("dve", lambda v, r0=r0, nb=nb: v.tensor_copy(
                          at_sb[:, r0 * 128:r0 * 128 + nb * 128],
                          self.psD[:, 0:nb * 128]), reads=["psD"], writes=["atb"])

                        def mmpv(t, r0=r0, nb=nb, tt=tt, jb=jb):
                            for b in range(nb):
                                ss = r0 + b
                                mm = t.matmul(self.psE[:], v_sb[jb][:, ss, :],
                                    at_sb[:, ss * 128:ss * 128 + 128],
                                    start=(ss == 0), stop=(ss == tt))
                            return mm
                        E("pe", mmpv, reads=["atb", kv], writes=["psE"], tag=f"pv{n}")
                    E("act", lambda a, n=n, tt=tt: a.activation(
                      self.big_sb.ap()[:, n, tt * 128:tt * 128 + 128], self.psE[:],
                      AF.Identity, bias=self.bq_cols[:, 32 + n:33 + n]),
                      reads=["psE", "bqc"], writes=[f"O{n}"])
                steps.append(step)
            return steps

        import os
        il = os.environ.get("P2IL", "1") == "1"
        p2sub = int(os.environ.get("P2SUB", "9"))
        p2nh = int(os.environ.get("P2NH", str(NH)))
        load_slab(0)
        for s_ in qkv_steps(0):
            s_()
        load_slab(1)
        for n in range(min(NH, p2nh)):
            if n + 2 < NH:
                load_slab(n + 2)
            at = attn_steps(n) if p2sub >= 2 else []
            qs = qkv_steps(n + 1) if n + 1 < NH else []
            if not il:
                for a_ in at:
                    a_()
                for q_ in qs:
                    q_()
            else:
                qi = 0
                for i, a_ in enumerate(at):
                    a_()
                    want = (i + 1) * len(qs) // len(at)
                    while qi < want:
                        qs[qi]()
                        qi += 1
            if tapO is not None:
                E("sp", lambda s, n=n: s.dma_start(
                  tapO[n * 128:(n + 1) * 128, :], self.big_sb.ap()[:, n, :]),
                  reads=[f"O{n}"], dma=True)

    def p3(self):
        E = self.emit
        self.inherit(["xb0", "xb1", "gld0", "gld1", "gf0", "gf1", "ob0", "ob1", "Bo"],
                     ["qsb0", "qsb1", "ksb0", "ksb1", "vsb0", "vsb1", "expS", "Asb", "atb"]
                     + [f"sl{j}_{c}{h}" for j in range(2) for c in range(3) for h in range(2)])
        xb = [self.wk(0, [128, DM], F32), self.wk(8192, [128, DM], F32)]
        gld = [self.wk(16384, [128, DM], BF16), self.wk(20480, [128, DM], BF16)]
        gf = [self.wk(24576, [128, DM], F32), self.wk(32768, [128, DM], F32)]
        ob = [self.wk(40960, [128, DM], F32), self.wk(49152, [128, DM], F32)]
        Bo = self.wk(57344, [128, DM], F32)
        self.inherit(["wo0", "wo1", "wo2", "wo3"],
                     [f"xn:{h}:0" for h in range(NT)] + [f"xn:{h}:1" for h in range(NT)])
        for q4 in range(4):
            E("pool", lambda g, q4=q4: g.dma_start(
              self.xn_sb.ap()[:, q4 * 4:q4 * 4 + 4, :],
              self.wo_d.rearrange("(ht p) h2 -> p ht h2", p=128)[:, q4 * 4:q4 * 4 + 4, :]),
              writes=[f"wo{q4}"], dma=True, tag="wo")
        E("sp", lambda s: s.dma_start(Bo,
          self.bo_d.rearrange("(o n) -> o n", o=1).partition_broadcast(128)),
          writes=["Bo"], dma=True)
        kwo = [f"wo{q4}" for q4 in range(4)]
        for tt in range(NT):
            j = tt % 2
            E("sp", lambda s, tt=tt, j=j: s.dma_start(xb[j],
              self.x_d[tt * 128:(tt + 1) * 128, :]), writes=[f"xb{j}"], dma=True)
            E("sp", lambda s, tt=tt, j=j: s.dma_start(gld[j],
              self.g_d[tt * 128:(tt + 1) * 128, :]),
              reads=[f"g:{tt}"], writes=[f"gld{j}"], dma=True)
            E("act", lambda a, j=j: a.activation(gf[j], gld[j], AF.Identity),
              reads=[f"gld{j}"], writes=[f"gf{j}"])
            for hh in range(2):
                pc = self.psC[:, hh * 1024:hh * 1024 + 1024]
                kbs = [f"psC{hh * 2}", f"psC{hh * 2 + 1}"]

                def mmop(t, tt=tt, hh=hh, pc=pc):
                    for hc in range(2):
                        for ht in range(NT):
                            mm = t.matmul(pc[:, hc * 512:hc * 512 + 512],
                                self.big_sb.ap()[:, ht, tt * 128:tt * 128 + 128],
                                self.xn_sb.ap()[:, ht, hh * 1024 + hc * 512:hh * 1024 + hc * 512 + 512],
                                start=(ht == 0), stop=(ht == NT - 1))
                    return mm
                E("pe", mmop, reads=kwo + [f"O{h}" for h in range(NH)],
                  writes=kbs, tag=f"op{tt}")
                obh = ob[j][:, hh * 1024:hh * 1024 + 1024]
                E("dve", lambda v, obh=obh, pc=pc, hh=hh: v.tensor_tensor(
                  obh, pc, Bo[:, hh * 1024:hh * 1024 + 1024], op=ALU.add),
                  reads=kbs + ["Bo"], writes=[f"ob{j}"])
                E("dve", lambda v, obh=obh, j=j, hh=hh: v.tensor_tensor(
                  obh, obh, gf[j][:, hh * 1024:hh * 1024 + 1024], op=ALU.mult),
                  reads=[f"ob{j}", f"gf{j}"], writes=[f"ob{j}"])
                E("dve", lambda v, obh=obh, j=j, hh=hh: v.tensor_tensor(
                  obh, obh, xb[j][:, hh * 1024:hh * 1024 + 1024], op=ALU.add),
                  reads=[f"ob{j}", f"xb{j}"], writes=[f"ob{j}"])
            E("sp", lambda s, tt=tt, j=j: s.dma_start(
              self.out_d[tt * 128:(tt + 1) * 128, :], ob[j]),
              reads=[f"ob{j}"], dma=True)


def build_program(taps=(), phases=4):
    p = P(taps=taps, phases=phases)
    nc = p.build()
    return nc, p


def kernel(L0_attn_x, L0_attn_Wqkv, L0_attn_bqkv, L0_attn_Wo, L0_attn_bo,
           L0_ffn_router_weight, L0_ffn_e_bias, L0_ffn_w_up, L0_ffn_w_down,
           L0_ffn_shared_up, L0_ffn_shared_down):
    nc, _ = build_program()
    common = {
        "Wqkv": np.ascontiguousarray(L0_attn_Wqkv, dtype=np.float32),
        "bqkv": np.ascontiguousarray(L0_attn_bqkv, dtype=np.float32),
        "Wo": np.ascontiguousarray(L0_attn_Wo, dtype=np.float32),
        "bo": np.ascontiguousarray(L0_attn_bo, dtype=np.float32),
        "rw": np.ascontiguousarray(L0_ffn_router_weight, dtype=np.float32),
        "e_bias": np.ascontiguousarray(L0_ffn_e_bias, dtype=np.float32),
        "w_up": np.ascontiguousarray(L0_ffn_w_up, dtype=np.float32),
        "w_down": np.ascontiguousarray(L0_ffn_w_down, dtype=np.float32),
        "sh_up": np.ascontiguousarray(L0_ffn_shared_up, dtype=np.float32),
        "sh_down": np.ascontiguousarray(L0_ffn_shared_down, dtype=np.float32),
    }
    in_maps = [dict(common, x=np.ascontiguousarray(L0_attn_x[b], dtype=np.float32))
               for b in range(8)]
    res = run_bass_kernel_spmd(nc, in_maps, core_ids=list(range(8)))
    return np.stack([np.asarray(r["out"]) for r in res.results])


# version 05
# speedup vs baseline: 1.2008x; 1.2008x over previous
import math
import numpy as np
import concourse.bass as bass
import concourse.mybir as mybir
from concourse.bass_utils import run_bass_kernel_spmd

F32 = mybir.dt.float32
BF16 = mybir.dt.bfloat16
AF = mybir.ActivationFunctionType
ALU = mybir.AluOpType
AX = mybir.AxisListType

S = 2048
DM = 2048
NH, HS = 16, 128
NE = 16
IM = 1024
NT = 16
RSQ = 1.0 / math.sqrt(HS)

ENGS = ("pe", "act", "dve", "pool", "sp")
BLKNAME = {"pe": "tensor", "act": "scalar", "dve": "vector", "pool": "gpsimd", "sp": "sync"}
NDMASEM = 8


class P:
    def __init__(self, taps=(), phases=4):
        self.taps = set(taps)
        self.phases = phases
        self.nc = bass.Bass("TRN2", target_bir_lowering=False)
        self.ops = {e: [] for e in ENGS}
        self.res = {}
        self.cnt = {}
        self.waited = {e: {} for e in ENGS}
        self.sems = {}
        self.sem_order = []
        for e in ENGS:
            self._newsem("c_" + e)
        for q in ("sp", "pool"):
            for i in range(NDMASEM):
                self._newsem(f"d_{q}{i}")
        self.dma_i = {"sp": 0, "pool": 0}
        self.tap_tensors = {}

    def _newsem(self, name):
        self.cnt[name] = 0
        self.sem_order.append(name)

    def emit(self, eng, fn, reads=(), writes=(), dma=False, tag=""):
        waits = {}

        def addw(tok):
            if tok is not None:
                s, v = tok
                if v > waits.get(s, 0):
                    waits[s] = v

        for k in reads:
            r = self.res.get(k)
            if r:
                addw(r[0])
        for k in writes:
            r = self.res.get(k)
            if r:
                addw(r[0])
                for t in r[1]:
                    addw(t)
        w = self.waited[eng]
        waits = {s: v for s, v in waits.items() if v > w.get(s, 0)}
        for s, v in waits.items():
            w[s] = v
        if dma:
            i = self.dma_i[eng]
            self.dma_i[eng] += 1
            sem = f"d_{eng}{i % NDMASEM}"
            step = 16
            if self.cnt[sem] > self.waited[eng].get(sem, 0):
                waits[sem] = self.cnt[sem]
                self.waited[eng][sem] = self.cnt[sem]
        else:
            sem = "c_" + eng
            step = 1
        self.cnt[sem] += step
        tok = (sem, self.cnt[sem])
        for k in writes:
            self.res[k] = [tok, []]
        for k in reads:
            r = self.res.setdefault(k, [None, []])
            r[1].append(tok)
        self.ops[eng].append((waits, fn, sem, step, tag))
        return tok

    def inherit(self, newkeys, oldkeys):
        toks = []
        for k in oldkeys:
            r = self.res.get(k)
            if r:
                if r[0]:
                    toks.append(r[0])
                toks.extend(r[1])
        for k in newkeys:
            r = self.res.setdefault(k, [None, []])
            r[1].extend(toks)

    def finish(self):
        for q in ("sp", "pool"):
            waits = {}
            for i in range(NDMASEM):
                s = f"d_{q}{i}"
                if self.cnt[s] > self.waited[q].get(s, 0):
                    waits[s] = self.cnt[s]
            if waits:
                self.ops[q].append((waits, None, None, 0, "final"))

    def realize(self):
        nc = self.nc
        from contextlib import ExitStack
        stack = ExitStack()
        for name in self.sem_order:
            self.sems[name] = stack.enter_context(nc.semaphore(name))
        blk = stack.enter_context(nc.Block())
        for eng in ENGS:
            ops = self.ops[eng]
            if not ops:
                continue

            def runner(handle, ops=ops):
                for waits, fn, sem, step, tag in ops:
                    for s, v in waits.items():
                        handle.wait_ge(self.sems[s], v)
                    if fn is not None:
                        fn(handle).then_inc(self.sems[sem], step)
            getattr(blk, BLKNAME[eng])(runner)
        stack.close()

    def tap(self, name, shape, dtype):
        if name in self.taps and name not in self.tap_tensors:
            self.tap_tensors[name] = self.nc.dram_tensor(
                "tap_" + name, shape, dtype, kind="ExternalOutput").ap()
        return self.tap_tensors.get(name)

    def _enter(self, cm):
        from contextlib import ExitStack
        if not hasattr(self, "_alloc_stack"):
            self._alloc_stack = ExitStack()
        return self._alloc_stack.enter_context(cm)

    def wk(self, off, shape, dt):
        n_el = 1
        for d in shape[1:]:
            n_el *= d
        nbytes = n_el * (4 if dt == F32 else 2)
        a = self.wk_sb.ap()[:, off // 2:(off + nbytes) // 2]
        if dt == F32:
            a = a.bitcast(F32)
        if len(shape) == 3:
            a = a.rearrange("p (a b) -> p a b", b=shape[2])
        elif len(shape) == 4:
            a = a.rearrange("p (a b c) -> p a b c", b=shape[2], c=shape[3])
        return a

    def build(self):
        nc = self.nc
        self.x_d = nc.dram_tensor("x", [S, DM], F32, kind="ExternalInput").ap()
        self.wqkv_d = nc.dram_tensor("Wqkv", [DM, 3 * DM], F32, kind="ExternalInput").ap()
        self.bqkv_d = nc.dram_tensor("bqkv", [3 * DM], F32, kind="ExternalInput").ap()
        self.wo_d = nc.dram_tensor("Wo", [DM, DM], F32, kind="ExternalInput").ap()
        self.bo_d = nc.dram_tensor("bo", [DM], F32, kind="ExternalInput").ap()
        self.rw_d = nc.dram_tensor("rw", [NE, DM], F32, kind="ExternalInput").ap()
        self.ebias_d = nc.dram_tensor("e_bias", [NE], F32, kind="ExternalInput").ap()
        self.wup_d = nc.dram_tensor("w_up", [NE, DM, IM], F32, kind="ExternalInput").ap()
        self.wdn_d = nc.dram_tensor("w_down", [NE, IM, DM], F32, kind="ExternalInput").ap()
        self.shup_d = nc.dram_tensor("sh_up", [DM, 4 * IM], F32, kind="ExternalInput").ap()
        self.shdn_d = nc.dram_tensor("sh_down", [4 * IM, DM], F32, kind="ExternalInput").ap()
        self.out_d = nc.dram_tensor("out", [S, DM], F32, kind="ExternalOutput").ap()
        self.g_d = nc.dram_tensor("g_scr", [S, DM], BF16, kind="Internal").ap()

        def sb(n, shp, dt):
            setattr(self, n, self._enter(nc.sbuf_tensor(n, shp, dt)))
        sb("xn_sb", [128, NT, DM], BF16)
        sb("big_sb", [128, NT, DM], BF16)
        sb("wk_sb", [128, 38912], BF16)
        sb("ident_f", [128, 128], F32)
        sb("ident_b", [128, 128], BF16)
        sb("mask_add", [128, 128], F32)
        sb("rwT", [128, NT, NE], F32)
        sb("bq_cols", [128, 48], F32)
        sb("bq_sc", [128, NH], F32)
        sb("ebias_bc", [128, NE], F32)
        sb("eidx", [128, NE], F32)
        sb("gidx", [128, 4], F32)
        sb("combine", [128, NT, NE], F32)
        sb("stats", [128, 32], F32)
        self.logits_ap = self.wk(73728, [128, NT, NE], F32)
        def bigc(off, shape):
            a = self.big_sb.ap().rearrange("p a b -> p (a b)")[:, off // 2:
                off // 2 + shape[-1] * (2 if len(shape) == 3 else 1) * 2]
            a = a.bitcast(F32)
            if len(shape) == 3:
                a = a.rearrange("p (a b) -> p a b", b=shape[2])
            return a
        self.cstage = bigc(0, [128, 128])[0:16, :]
        self.rst = bigc(512, [128, 2, 128])[0:16, :, :]
        self.bstage = bigc(2560, [128, 128])[0:48, :]
        self.psA = self._enter(nc.psum_tensor("psA", [128, 512], F32))
        self.psB = self._enter(nc.psum_tensor("psB", [128, 512], F32))
        self.psC = self._enter(nc.psum_tensor("psC", [128, 2048], F32))
        self.psD = self._enter(nc.psum_tensor("psD", [128, 1024], BF16))
        self.psE = self._enter(nc.psum_tensor("psE", [128, 128], F32))

        self.setup()
        self.p0()
        if self.phases >= 2:
            self.p1()
        if self.phases >= 3:
            self.p2()
        if self.phases >= 4:
            self.p3()
        self.finish()
        self.realize()
        return nc

    def setup(self):
        E = self.emit
        E("pool", lambda g: g.memset(self.ident_f[:], 0.0), writes=["idf"])
        E("pool", lambda g: g.affine_select(out=self.ident_f[:], in_=self.ident_f[:],
          compare_op=ALU.not_equal, fill=1.0, base=0, pattern=[[-1, 128]],
          channel_multiplier=1), reads=["idf"], writes=["idf"])
        E("pool", lambda g: g.memset(self.ident_b[:], 0.0), writes=["idb"])
        E("pool", lambda g: g.affine_select(out=self.ident_b[:], in_=self.ident_b[:],
          compare_op=ALU.not_equal, fill=1.0, base=0, pattern=[[-1, 128]],
          channel_multiplier=1), reads=["idb"], writes=["idb"])
        E("pool", lambda g: g.memset(self.mask_add[:], 0.0), writes=["mask"])
        E("pool", lambda g: g.affine_select(out=self.mask_add[:], in_=self.mask_add[:],
          compare_op=ALU.is_ge, fill=-30000.0, base=0, pattern=[[-1, 128]],
          channel_multiplier=1), reads=["mask"], writes=["mask"])
        E("pool", lambda g: g.iota(self.eidx[:], pattern=[[1, NE]], base=-99,
          channel_multiplier=0, allow_small_or_imprecise_dtypes=True), writes=["eidx"])
        E("pool", lambda g: g.iota(self.gidx[:], pattern=[[1, 4]], base=-99,
          channel_multiplier=0, allow_small_or_imprecise_dtypes=True), writes=["gidx"])
        E("sp", lambda s: s.dma_start(self.ebias_bc[:],
          self.ebias_d.rearrange("(o n) -> o n", o=1).partition_broadcast(128)),
          writes=["ebias"], dma=True)
        E("sp", lambda s: s.dma_start(self.bstage[:],
          self.bqkv_d.rearrange("(a b) -> a b", b=128)), writes=["bstage"], dma=True)
        E("pe", lambda t: t.transpose(self.psA[:, 0:48], self.bstage[:],
          self.ident_f[0:48, 0:48]), reads=["bstage", "idf"], writes=["psA"])
        E("dve", lambda v: v.tensor_copy(self.bq_cols[:], self.psA[:, 0:48]),
          reads=["psA"], writes=["bqc"])
        E("dve", lambda v: v.tensor_scalar(self.bq_sc[:], self.bq_cols[:, 0:16],
          RSQ, None, op0=ALU.mult), reads=["bqc"], writes=["bqsc"])
        for ht in range(NT):
            j = ht % 2
            E("sp", lambda s, ht=ht, j=j: s.dma_start(self.rst[:, j, :],
              self.rw_d[:, ht * 128:(ht + 1) * 128]), writes=[f"rst{j}"], dma=True)
            E("pe", lambda t, j=j: t.transpose(self.psB[:, 0:16],
              self.rst[:, j, :], self.ident_f[0:16, 0:16]),
              reads=[f"rst{j}", "idf"], writes=["psB"])
            E("dve", lambda v, ht=ht: v.tensor_copy(self.rwT.ap()[:, ht, :],
              self.psB[:, 0:16]), reads=["psB"], writes=["rwT"])

    def p0(self):
        import os
        sub = int(os.environ.get("P0SUB", "9"))
        E = self.emit
        xbuf = [self.wk(0, [128, DM], F32), self.wk(8192, [128, DM], F32)]
        xnb = [self.wk(16384, [128, DM], F32), self.wk(24576, [128, DM], F32)]
        dump = self.wk(32768, [128, DM], F32)
        dump2 = self.wk(43008, [128, DM], F32)
        xt32 = [self.wk(40960, [128, 256], F32), self.wk(41984, [128, 256], F32)]
        st = self.stats.ap()
        tapxn = self.tap("xn", [S, DM], F32)

        for tp in range(8):
            for j in range(2):
                tt = tp * 2 + j
                xb, xn = xbuf[j], xnb[j]
                kx, kxn = f"xb{j}", f"xn{j}"
                E("sp", lambda s, tt=tt, xb=xb: s.dma_start(
                    xb, self.x_d[tt * 128:(tt + 1) * 128, :]),
                    writes=[kx], dma=True, tag=f"ldx{tt}")
                o = j * 5
                dmp = [dump, dump2][j]
                kd = f"dump{j}"
                E("dve", lambda v, xb=xb, o=o: v.reduce_sum(st[:, o:o + 1], xb,
                  axis=AX.X, negate=True), reads=[kx], writes=[f"st{o}"])
                E("act", lambda a, xb=xb, o=o, dmp=dmp: a.activation(dmp, xb,
                  AF.Square, accum_out=st[:, o + 1:o + 2]),
                  reads=[kx], writes=[kd, f"st{o+1}"])
                E("dve", lambda v, o=o: v.tensor_scalar(st[:, o:o + 1], st[:, o:o + 1],
                  1.0 / DM, None, op0=ALU.mult), reads=[f"st{o}"], writes=[f"st{o}"])
                E("dve", lambda v, o=o: v.tensor_scalar(st[:, o + 1:o + 2], st[:, o + 1:o + 2],
                  1.0 / DM, None, op0=ALU.mult), reads=[f"st{o+1}"], writes=[f"st{o+1}"])
                E("dve", lambda v, o=o: v.tensor_tensor(st[:, o + 2:o + 3], st[:, o:o + 1],
                  st[:, o:o + 1], op=ALU.mult), reads=[f"st{o}"], writes=[f"st{o+2}"])
                E("dve", lambda v, o=o: v.tensor_tensor(st[:, o + 2:o + 3], st[:, o + 1:o + 2],
                  st[:, o + 2:o + 3], op=ALU.subtract), reads=[f"st{o+1}", f"st{o+2}"], writes=[f"st{o+2}"])
                E("dve", lambda v, o=o: v.tensor_scalar(st[:, o + 2:o + 3], st[:, o + 2:o + 3],
                  1e-5, None, op0=ALU.add), reads=[f"st{o+2}"], writes=[f"st{o+2}"])
                E("act", lambda a, o=o: a.activation(st[:, o + 3:o + 4], st[:, o + 2:o + 3], AF.Ln),
                  reads=[f"st{o+2}"], writes=[f"st{o+3}"])
                E("act", lambda a, o=o: a.activation(st[:, o + 3:o + 4], st[:, o + 3:o + 4], AF.Exp,
                  scale=-0.5), reads=[f"st{o+3}"], writes=[f"st{o+3}"])
                E("dve", lambda v, o=o: v.tensor_tensor(st[:, o + 4:o + 5], st[:, o:o + 1],
                  st[:, o + 3:o + 4], op=ALU.mult), reads=[f"st{o}", f"st{o+3}"], writes=[f"st{o+4}"])
                E("act", lambda a, xb=xb, xn=xn, o=o: a.activation(xn, xb, AF.Identity,
                  bias=st[:, o + 4:o + 5], scale=st[:, o + 3:o + 4]),
                  reads=[kx, f"st{o+3}", f"st{o+4}"], writes=[kxn])
                if tapxn is not None:
                    E("sp", lambda s, tt=tt, xn=xn: s.dma_start(
                        tapxn[tt * 128:(tt + 1) * 128, :], xn), reads=[kxn], dma=True)
            if sub < 2:
                continue
            for ht in range(NT):
                bank = ht % 4
                pc = self.psC[:, bank * 512:bank * 512 + 512]
                kb = f"psC{bank}"
                E("pe", lambda t, ht=ht, pc=pc: t.transpose(
                    pc[:, 0:128], xnb[0][:, ht * 128:(ht + 1) * 128], self.ident_f[:]),
                    reads=["xn0", "idf"], writes=[kb])
                E("pe", lambda t, ht=ht, pc=pc: t.transpose(
                    pc[:, 128:256], xnb[1][:, ht * 128:(ht + 1) * 128], self.ident_f[:]),
                    reads=["xn1", "idf"], writes=[kb])
                E("dve", lambda v, ht=ht, tp=tp, pc=pc: v.tensor_copy(
                    self.xn_sb.ap()[:, ht, tp * 256:tp * 256 + 256], pc[:, 0:256]),
                    reads=[kb], writes=[f"xn:{ht}:{tp // 4}"])
                j2 = ht % 2
                if sub >= 3:
                    E("act", lambda a, pc=pc, j2=j2: a.activation(xt32[j2], pc[:, 0:256], AF.Identity),
                        reads=[kb, f"xn:{ht}:{tp // 4}"], writes=[f"xt32{j2}"])
                if sub >= 4:
                    E("pe", lambda t, ht=ht, j2=j2: t.matmul(
                        self.psA[0:16, 0:256], self.rwT.ap()[:, ht, :], xt32[j2],
                        start=(ht == 0), stop=(ht == NT - 1)),
                        reads=[f"xt32{j2}", "rwT"], writes=["psA"])
            if sub < 5:
                continue
            for j in range(2):
                E("dve", lambda v, j=j: v.tensor_copy(self.cstage[:, 0:128],
                  self.psA[0:16, j * 128:j * 128 + 128]),
                  reads=["psA"], writes=["cst"])
                E("pe", lambda t: t.transpose(self.psB[:, 0:16],
                  self.cstage[:, 0:128], self.ident_f[0:16, 0:16]),
                  reads=["cst", "idf"], writes=["psB"])
                E("dve", lambda v, tp=tp, j=j: v.tensor_copy(
                  self.logits_ap[:, tp * 2 + j, :], self.psB[:, 0:16]),
                  reads=["psB"], writes=["logits"])
        if sub < 6:
            return
        L = self.logits_ap
        r0 = self.wk(65536, [128, NT, NE], F32)
        s_ = self.wk(66560, [128, NT, NE], F32)
        mk = self.wk(67584, [128, NT, NE], F32)
        rm1 = self.wk(68608, [128, NT, NE], F32)
        rm2 = self.wk(69632, [128, NT, NE], F32)
        gs = self.wk(70656, [128, NT, 4], F32)
        gt = self.wk(70912, [128, NT, 4], F32)
        g1m = self.wk(71168, [128, NT, 4], F32)
        g2m = self.wk(71424, [128, NT, 4], F32)
        st2 = self.wk(71680, [128, 12, NT], F32)

        def D(fn, reads=(), writes=()):
            E("dve", fn, reads=list(reads) + ["sel"], writes=list(writes) + ["sel"])

        E("act", lambda a: a.activation(r0.rearrange("p a b -> p (a b)"),
          L.rearrange("p a b -> p (a b)"), AF.Sigmoid),
          reads=["logits", "sel"], writes=["r0sel", "sel"])
        eb = self.ebias_bc.ap().unsqueeze(1).broadcast_to([128, NT, NE])
        D(lambda v: v.tensor_tensor(s_, r0, eb, op=ALU.add),
          reads=["r0sel", "ebias"])
        s4 = s_.rearrange("p t (g e) -> p t g e", e=4)
        pairs = [(0, 1), (0, 2), (0, 3), (1, 2), (1, 3), (2, 3)]
        for pi, (i, j) in enumerate(pairs):
            dst = gs if pi == 0 else gt
            D(lambda v, i=i, j=j, dst=dst: v.tensor_tensor(
              dst, s4[:, :, :, i], s4[:, :, :, j], op=ALU.add))
            if pi:
                D(lambda v: v.tensor_tensor(gs, gs, gt, op=ALU.max))
        gidxb = self.gidx.ap().unsqueeze(1).broadcast_to([128, NT, 4])
        eidxb = self.eidx.ap().unsqueeze(1).broadcast_to([128, NT, NE])

        def pick(vec, idxb, mrow, irow, dstm, width, kidx):
            mb = st2[:, mrow, :].unsqueeze(2).broadcast_to([128, NT, width])
            ib = st2[:, irow, :].unsqueeze(2).broadcast_to([128, NT, width])
            D(lambda v: v.tensor_reduce(st2[:, mrow, :], vec, axis=AX.X,
              op=ALU.max))
            D(lambda v: v.tensor_tensor(dstm, vec, mb, op=ALU.is_ge))
            D(lambda v: v.tensor_tensor(dstm, dstm, idxb, op=ALU.mult),
              reads=[kidx])
            D(lambda v: v.tensor_scalar(dstm, dstm, 99.0, None, op0=ALU.add))
            D(lambda v: v.tensor_reduce(st2[:, irow, :], dstm, axis=AX.X,
              op=ALU.min))
            D(lambda v: v.tensor_tensor(dstm, dstm, ib, op=ALU.is_equal))

        pick(gs, gidxb, 0, 1, g1m, 4, "gidx")
        D(lambda v: v.tensor_scalar(gt, g1m, -1e30, None, op0=ALU.mult))
        D(lambda v: v.tensor_tensor(gt, gs, gt, op=ALU.add))
        pick(gt, gidxb, 2, 3, g2m, 4, "gidx")
        D(lambda v: v.tensor_tensor(g1m, g1m, g2m, op=ALU.add))
        mk4 = mk.rearrange("p t (g e) -> p t g e", e=4)
        for jj in range(4):
            D(lambda v, jj=jj: v.tensor_tensor(
              mk4[:, :, :, jj], s4[:, :, :, jj], g1m, op=ALU.mult))
        pick(mk, eidxb, 4, 5, rm1, NE, "eidx")
        D(lambda v: v.tensor_scalar(s_, rm1, -1e30, None, op0=ALU.mult))
        D(lambda v: v.tensor_tensor(s_, mk, s_, op=ALU.add))
        pick(s_, eidxb, 6, 7, rm2, NE, "eidx")
        D(lambda v: v.tensor_tensor(s_, r0, rm1, op=ALU.mult))
        D(lambda v: v.tensor_reduce(st2[:, 8, :], s_, axis=AX.X, op=ALU.add))
        D(lambda v: v.tensor_tensor(s_, r0, rm2, op=ALU.mult))
        D(lambda v: v.tensor_reduce(st2[:, 9, :], s_, axis=AX.X, op=ALU.add))
        D(lambda v: v.tensor_tensor(st2[:, 10, :], st2[:, 8, :],
          st2[:, 9, :], op=ALU.add))
        D(lambda v: v.tensor_scalar(st2[:, 10, :], st2[:, 10, :], 1e-20,
          None, op0=ALU.add))
        D(lambda v: v.reciprocal(st2[:, 11, :], st2[:, 10, :]))
        D(lambda v: v.tensor_tensor(st2[:, 8, :], st2[:, 8, :],
          st2[:, 11, :], op=ALU.mult))
        D(lambda v: v.tensor_tensor(st2[:, 9, :], st2[:, 9, :],
          st2[:, 11, :], op=ALU.mult))
        C = self.combine.ap()
        w1b = st2[:, 8, :].unsqueeze(2).broadcast_to([128, NT, NE])
        w2b = st2[:, 9, :].unsqueeze(2).broadcast_to([128, NT, NE])
        D(lambda v: v.tensor_tensor(C, rm1, w1b, op=ALU.mult))
        D(lambda v: v.tensor_tensor(s_, rm2, w2b, op=ALU.mult))
        D(lambda v: v.tensor_tensor(C, C, s_, op=ALU.add), writes=["comb"])
        taplog = self.tap("logits", [S, NE], F32)
        tapcmb = self.tap("combine", [S, NE], F32)
        if taplog is not None:
            for tt in range(NT):
                E("sp", lambda s, tt=tt: s.dma_start(
                  taplog[tt * 128:(tt + 1) * 128, :], L[:, tt, :]),
                  reads=["logits", "sel"], dma=True)
                E("sp", lambda s, tt=tt: s.dma_start(
                  tapcmb[tt * 128:(tt + 1) * 128, :], C[:, tt, :]),
                  reads=["comb"], dma=True)

    def p1(self):
        E = self.emit
        self.inherit(["Aebuf", "wdnh0", "wdnh1", "wup0a", "wup0b", "wup1a",
                      "wup1b", "wup2a", "wup2b", "tmp0", "tmp1",
                      "scr1a", "scr1b", "gst"],
                     ["xb0", "xb1", "xn0", "xn1", "dump0", "dump1", "xt320",
                      "xt321", "logits", "sel"])
        self.inherit([f"rt{tt}{hh}" for tt in range(8) for hh in range(2)],
                     ["cst", "bstage", "rst0", "rst1"])
        Ae = self.wk(0, [128, 8, IM], BF16)
        wdnh = [self.wk(16384, [128, 8, 1024], BF16),
                self.wk(32768, [128, 8, 1024], BF16)]
        wup = [self.wk(49152, [128, NT, 128], BF16),
               self.wk(53248, [128, NT, 128], BF16),
               self.wk(57344, [128, NT, 128], BF16)]
        scr1 = [self.wk(61440, [128, 512], F32), self.wk(63488, [128, 512], F32)]
        tmp = [self.wk(65536, [128, 1024], F32), self.wk(69632, [128, 1024], F32)]
        gst = self.wk(73728, [128, DM], BF16)
        routed = self.big_sb.ap().rearrange("p a b -> p (a b)").bitcast(F32) \
                     .rearrange("p (tt h) -> p tt h", h=DM)
        comb = self.combine.ap()

        order_e = [NE, NE + 1, NE + 2, NE + 3] + list(range(NE))
        chunks = [(H, e, ic) for H in range(2) for e in order_e
                  for ic in range(8)]
        dhalves = [(H, e, hh) for H in range(2) for e in order_e
                   for hh in range(2)]

        def load_up(c):
            _, e, ic = chunks[c]
            wb = wup[c % 3]
            if e < NE:
                src = self.wup_d[e, :, ic * 128:(ic + 1) * 128]
            else:
                jj = e - NE
                src = self.shup_d[:, jj * IM + ic * 128:jj * IM + (ic + 1) * 128]
            for half in range(2):
                E("pool", lambda g, src=src, wb=wb, half=half: g.dma_start(
                  wb[:, half * 8:half * 8 + 8, :],
                  src.rearrange("(ht p) c -> p ht c", p=128)[:, half * 8:half * 8 + 8, :]),
                  writes=[f"wup{c % 3}{'ab'[half]}"], dma=True)

        def load_dn(d):
            _, e, hh = dhalves[d]
            if e < NE:
                src2 = self.wdn_d[e, :, hh * 1024:hh * 1024 + 1024]
            else:
                jj = e - NE
                src2 = self.shdn_d[jj * IM:(jj + 1) * IM, hh * 1024:hh * 1024 + 1024]
            sr = src2.rearrange("(ii p) h -> p ii h", p=128)
            for q in range(4):
                E("pool", lambda g, sr=sr, d=d, q=q: g.dma_start(
                  wdnh[d % 2][:, q * 2:q * 2 + 2, :], sr[:, q * 2:q * 2 + 2, :]),
                  writes=[f"wdnh{d % 2}"], dma=True)

        for c in range(3):
            load_up(c)
        load_dn(0)
        load_dn(1)
        ci, di = 3, 2
        for H in range(2):
            t0 = H * 1024
            for oi, e in enumerate(order_e):
                gi = H * 20 + oi
                moe = e < NE
                for ic in range(8):
                    c = gi * 8 + ic
                    wb = wup[c % 3]
                    kwb = [f"wup{c % 3}a", f"wup{c % 3}b"]
                    for tc in range(2):
                        ps = [self.psA, self.psB][tc]
                        kps = ["psA", "psB"][tc]

                        def mmup(t, wb=wb, ps=ps, t0=t0, tc=tc):
                            for ht in range(NT):
                                mm = t.matmul(ps[:], wb[:, ht, :],
                                    self.xn_sb.ap()[:, ht, t0 + tc * 512:t0 + tc * 512 + 512],
                                    start=(ht == 0), stop=(ht == NT - 1))
                            return mm
                        E("pe", mmup, reads=kwb + [f"xn:{h}:{H}" for h in range(NT)],
                          writes=[kps], tag=f"up{e}")
                        s1 = scr1[tc]
                        ks1 = f"scr1{'ab'[tc]}"
                        E("act", lambda a, ps=ps, s1=s1: a.activation(s1, ps[:],
                          AF.Relu), reads=[kps], writes=[ks1])
                        dst = Ae[:, ic, tc * 512:tc * 512 + 512]
                        E("act", lambda a, s1=s1, dst=dst: a.activation(
                          dst, s1, AF.Square), reads=[ks1], writes=["Aebuf"])
                    if ci < len(chunks):
                        load_up(ci)
                        ci += 1
                for hh in range(2):
                    d = gi * 2 + hh
                    kwd = f"wdnh{d % 2}"
                    for tt in range(8):
                        bp = tt % 2
                        pc = self.psC[:, bp * 1024:bp * 1024 + 1024]
                        kbs = [f"psC{bp * 2}", f"psC{bp * 2 + 1}"]

                        def mmdn(t, tt=tt, pc=pc, d=d):
                            for hc in range(2):
                                for ii in range(8):
                                    mm = t.matmul(
                                        pc[:, hc * 512:hc * 512 + 512],
                                        Ae[:, ii, tt * 128:tt * 128 + 128],
                                        wdnh[d % 2][:, ii, hc * 512:hc * 512 + 512],
                                        start=(ii == 0), stop=(ii == 7))
                            return mm
                        E("pe", mmdn, reads=["Aebuf", kwd],
                          writes=kbs, tag=f"dn{e}")
                        dstr = routed[:, tt, hh * 1024:hh * 1024 + 1024]
                        krt = f"rt{tt}{hh}"
                        if moe:
                            tb = tmp[tt % 2]
                            ktb = f"tmp{tt % 2}"
                            E("act", lambda a, pc=pc, tb=tb, e=e, g=H * 8 + tt:
                              a.activation(tb, pc, AF.Identity,
                              scale=comb[:, g, e:e + 1]),
                              reads=kbs + ["comb"], writes=[ktb])
                            E("dve", lambda v, dstr=dstr, tb=tb: v.tensor_tensor(
                              dstr, dstr, tb, op=ALU.add),
                              reads=[ktb, krt], writes=[krt])
                        elif oi == 0:
                            E("dve", lambda v, dstr=dstr, pc=pc: v.tensor_copy(
                              dstr, pc), reads=kbs, writes=[krt])
                        else:
                            E("dve", lambda v, dstr=dstr, pc=pc: v.tensor_tensor(
                              dstr, dstr, pc, op=ALU.add),
                              reads=kbs + [krt], writes=[krt])
                    if di < len(dhalves):
                        load_dn(di)
                        di += 1
            taprt = self.tap("routed", [S, DM], F32)
            tapg = self.tap("g", [S, DM], BF16)
            for tt in range(8):
                gt = H * 8 + tt
                if taprt is not None:
                    E("sp", lambda s, gt=gt, tt=tt: s.dma_start(
                      taprt[gt * 128:(gt + 1) * 128, :], routed[:, tt, :]),
                      reads=[f"rt{tt}0", f"rt{tt}1"], dma=True)
                E("act", lambda a, tt=tt: a.activation(gst, routed[:, tt, :],
                  AF.Silu), reads=[f"rt{tt}0", f"rt{tt}1", "gst"], writes=["gst"])
                E("sp", lambda s, gt=gt: s.dma_start(
                  self.g_d[gt * 128:(gt + 1) * 128, :], gst),
                  reads=["gst"], writes=[f"g:{gt}"], dma=True)
                if tapg is not None:
                    E("sp", lambda s, gt=gt: s.dma_start(
                      tapg[gt * 128:(gt + 1) * 128, :], gst),
                      reads=["gst"], dma=True)

    def p2(self):
        import os
        p2tap = int(os.environ.get("P2TAP", "-1"))
        E = self.emit
        self.inherit(["qsb", "ksb", "vsb", "expS", "Asb", "atb"]
                     + [f"sl{j}_{c}{h}" for j in range(2) for c in range(3) for h in range(2)],
                     ["Aebuf", "wdnh0", "wdnh1", "wup0a", "wup0b", "wup1a",
                      "wup1b", "wup2a", "wup2b", "tmp0", "tmp1",
                      "scr1a", "scr1b", "gst"])
        q_sb = self.wk(0, [128, S], BF16)
        k_sb = self.wk(4096, [128, S], BF16)
        v_sb = self.wk(8192, [128, NT, HS], BF16)
        slab = [self.wk(12288, [128, 3, NT, 128], BF16),
                self.wk(24576, [128, 3, NT, 128], BF16)]
        expS = self.wk(36864, [128, S], F32)
        A_sb = self.wk(45056, [128, S], BF16)
        at_sb = self.wk(49152, [128, S], BF16)
        st = self.stats.ap()
        tapO = self.tap("O", [S, DM], BF16)

        def load_slab(n, jb):
            for c in range(3):
                for half in range(2):
                    E("pool", lambda g, n=n, c=c, half=half, jb=jb: g.dma_start(
                      slab[jb][:, c, half * 8:half * 8 + 8, :],
                      self.wqkv_d[:, c * DM + n * 128:c * DM + (n + 1) * 128]
                      .rearrange("(ht p) c2 -> p ht c2", p=128)[:, half * 8:half * 8 + 8, :]),
                      writes=[f"sl{jb}_{c}{half}"], dma=True, tag=f"slab{n}")

        load_slab(0, 0)
        for n in range(NH):
            jb = n % 2
            kslab = [f"sl{jb}_{c}{h}" for c in range(3) for h in range(2)]
            if n + 1 < NH:
                load_slab(n + 1, (n + 1) % 2)
            for c, dst, kdst, bias, scale in (
                    (0, q_sb, "qsb", self.bq_sc[:, n:n + 1], RSQ),
                    (1, k_sb, "ksb", self.bq_cols[:, 16 + n:17 + n], 1.0)):
                for tc in range(4):
                    ps = [self.psA, self.psB][tc % 2]
                    kps = ["psA", "psB"][tc % 2]

                    def mmqk(t, jb=jb, c=c, ps=ps, tc=tc):
                        for ht in range(NT):
                            mm = t.matmul(ps[:], slab[jb][:, c, ht, :],
                                self.xn_sb.ap()[:, ht, tc * 512:tc * 512 + 512],
                                start=(ht == 0), stop=(ht == NT - 1))
                        return mm
                    E("pe", mmqk, reads=kslab + [f"xn:{h}:0" for h in range(NT)] + [f"xn:{h}:1" for h in range(NT)],
                      writes=[kps], tag=f"qk{n}")
                    E("act", lambda a, ps=ps, dst=dst, tc=tc, bias=bias, scale=scale:
                      a.activation(dst[:, tc * 512:tc * 512 + 512], ps[:], AF.Identity,
                      bias=bias, scale=scale),
                      reads=[kps, "bqc", "bqsc"], writes=[kdst])
            for tt in range(NT):
                psv = [self.psA, self.psB][tt % 2]
                kpsv = ["psA", "psB"][tt % 2]

                def mmv(t, jb=jb, tt=tt, psv=psv):
                    for ht in range(NT):
                        mm = t.matmul(psv[:, 0:128],
                            self.xn_sb.ap()[:, ht, tt * 128:tt * 128 + 128],
                            slab[jb][:, 2, ht, :], start=(ht == 0), stop=(ht == NT - 1))
                    return mm
                E("pe", mmv, reads=kslab + [f"xn:{h}:0" for h in range(NT)] + [f"xn:{h}:1" for h in range(NT)],
                  writes=[kpsv], tag=f"v{n}")
                E("dve", lambda v, tt=tt, psv=psv: v.tensor_copy(
                  v_sb[:, tt, :], psv[:, 0:128]), reads=[kpsv], writes=["vsb"])
            if n == p2tap:
                tq = self.tap("q", [128, S], BF16)
                tk = self.tap("k", [128, S], BF16)
                tv = self.tap("v", [S, HS], BF16)
                if tq is not None:
                    E("sp", lambda s, tq=tq: s.dma_start(tq, q_sb), reads=["qsb"], dma=True)
                    E("sp", lambda s, tk=tk: s.dma_start(tk, k_sb), reads=["ksb"], dma=True)
                    for tt in range(NT):
                        E("sp", lambda s, tt=tt, tv=tv: s.dma_start(
                          tv[tt * 128:(tt + 1) * 128, :], v_sb[:, tt, :]),
                          reads=["vsb"], dma=True)
            for tt in range(NT):
                ln = (tt + 1) * 128
                nch = (ln + 511) // 512
                for ch in range(nch):
                    w = min(512, ln - ch * 512)
                    pc = self.psC[:, ch * 512:ch * 512 + w]
                    kb = f"psC{ch}"
                    E("pe", lambda t, tt=tt, w=w, ch=ch, pc=pc: t.matmul(
                      pc, q_sb[:, tt * 128:tt * 128 + 128],
                      k_sb[:, ch * 512:ch * 512 + w], start=True, stop=True),
                      reads=["qsb", "ksb"], writes=[kb], tag=f"s{n}")
                    if ch == tt // 4:
                        off = (tt % 4) * 128
                        E("dve", lambda v, pc=pc, off=off: v.tensor_tensor(
                          pc[:, off:off + 128], pc[:, off:off + 128],
                          self.mask_add[:], op=ALU.add),
                          reads=[kb, "mask"], writes=[kb])
                    E("act", lambda a, pc=pc, ch=ch, w=w: a.activation(
                      expS[:, ch * 512:ch * 512 + w], pc, AF.Exp,
                      accum_out=st[:, 25 + ch:26 + ch]),
                      reads=[kb], writes=["expS", f"st{25 + ch}"])
                for ch in range(1, nch):
                    E("dve", lambda v, ch=ch: v.tensor_tensor(st[:, 25:26],
                      st[:, 25:26], st[:, 25 + ch:26 + ch], op=ALU.add),
                      reads=["st25", f"st{25 + ch}"], writes=["st25"])
                E("dve", lambda v: v.reciprocal(st[:, 30:31], st[:, 25:26]),
                  reads=["st25"], writes=["st30"])
                E("dve", lambda v, ln=ln: v.tensor_scalar(A_sb[:, 0:ln],
                  expS[:, 0:ln], st[:, 30:31], None, op0=ALU.mult),
                  reads=["expS", "st30"], writes=["Asb"])
                if n == p2tap and tt == NT - 1:
                    tA = self.tap("A", [128, S], BF16)
                    if tA is not None:
                        E("sp", lambda s, tA=tA: s.dma_start(tA, A_sb),
                          reads=["Asb"], dma=True)
                nss = tt + 1
                for r0 in range(0, nss, 8):
                    nb = min(8, nss - r0)

                    def mmtp(t, r0=r0, nb=nb):
                        for b in range(nb):
                            mm = t.transpose(self.psD[:, b * 128:b * 128 + 128],
                                A_sb[:, (r0 + b) * 128:(r0 + b) * 128 + 128],
                                self.ident_b[:])
                        return mm
                    E("pe", mmtp, reads=["Asb", "idb"], writes=["psD"], tag=f"tp{n}")
                    E("dve", lambda v, r0=r0, nb=nb: v.tensor_copy(
                      at_sb[:, r0 * 128:r0 * 128 + nb * 128],
                      self.psD[:, 0:nb * 128]), reads=["psD"], writes=["atb"])

                    def mmpv(t, r0=r0, nb=nb, tt=tt):
                        for b in range(nb):
                            ss = r0 + b
                            mm = t.matmul(self.psE[:], v_sb[:, ss, :],
                                at_sb[:, ss * 128:ss * 128 + 128],
                                start=(ss == 0), stop=(ss == tt))
                        return mm
                    E("pe", mmpv, reads=["atb", "vsb"], writes=["psE"], tag=f"pv{n}")
                E("act", lambda a, n=n, tt=tt: a.activation(
                  self.big_sb.ap()[:, n, tt * 128:tt * 128 + 128], self.psE[:],
                  AF.Identity, bias=self.bq_cols[:, 32 + n:33 + n]),
                  reads=["psE", "bqc"], writes=[f"O{n}"])
            if tapO is not None:
                E("sp", lambda s, n=n: s.dma_start(
                  tapO[n * 128:(n + 1) * 128, :], self.big_sb.ap()[:, n, :]),
                  reads=[f"O{n}"], dma=True)

    def p3(self):
        E = self.emit
        self.inherit(["xb0", "xb1", "gld0", "gld1", "gf0", "gf1", "ob0", "ob1", "Bo"],
                     ["qsb", "ksb", "vsb", "expS", "Asb", "atb"]
                     + [f"sl{j}_{c}{h}" for j in range(2) for c in range(3) for h in range(2)])
        xb = [self.wk(0, [128, DM], F32), self.wk(8192, [128, DM], F32)]
        gld = [self.wk(16384, [128, DM], BF16), self.wk(20480, [128, DM], BF16)]
        gf = [self.wk(24576, [128, DM], F32), self.wk(32768, [128, DM], F32)]
        ob = [self.wk(40960, [128, DM], F32), self.wk(49152, [128, DM], F32)]
        Bo = self.wk(57344, [128, DM], F32)
        self.inherit(["wo0", "wo1", "wo2", "wo3"],
                     [f"xn:{h}:0" for h in range(NT)] + [f"xn:{h}:1" for h in range(NT)])
        for q4 in range(4):
            E("pool", lambda g, q4=q4: g.dma_start(
              self.xn_sb.ap()[:, q4 * 4:q4 * 4 + 4, :],
              self.wo_d.rearrange("(ht p) h2 -> p ht h2", p=128)[:, q4 * 4:q4 * 4 + 4, :]),
              writes=[f"wo{q4}"], dma=True, tag="wo")
        E("sp", lambda s: s.dma_start(Bo,
          self.bo_d.rearrange("(o n) -> o n", o=1).partition_broadcast(128)),
          writes=["Bo"], dma=True)
        kwo = [f"wo{q4}" for q4 in range(4)]
        for tt in range(NT):
            j = tt % 2
            E("sp", lambda s, tt=tt, j=j: s.dma_start(xb[j],
              self.x_d[tt * 128:(tt + 1) * 128, :]), writes=[f"xb{j}"], dma=True)
            E("sp", lambda s, tt=tt, j=j: s.dma_start(gld[j],
              self.g_d[tt * 128:(tt + 1) * 128, :]),
              reads=[f"g:{tt}"], writes=[f"gld{j}"], dma=True)
            E("act", lambda a, j=j: a.activation(gf[j], gld[j], AF.Identity),
              reads=[f"gld{j}"], writes=[f"gf{j}"])
            for hh in range(2):
                pc = self.psC[:, hh * 1024:hh * 1024 + 1024]
                kbs = [f"psC{hh * 2}", f"psC{hh * 2 + 1}"]

                def mmop(t, tt=tt, hh=hh, pc=pc):
                    for hc in range(2):
                        for ht in range(NT):
                            mm = t.matmul(pc[:, hc * 512:hc * 512 + 512],
                                self.big_sb.ap()[:, ht, tt * 128:tt * 128 + 128],
                                self.xn_sb.ap()[:, ht, hh * 1024 + hc * 512:hh * 1024 + hc * 512 + 512],
                                start=(ht == 0), stop=(ht == NT - 1))
                    return mm
                E("pe", mmop, reads=kwo + [f"O{h}" for h in range(NH)],
                  writes=kbs, tag=f"op{tt}")
                obh = ob[j][:, hh * 1024:hh * 1024 + 1024]
                E("dve", lambda v, obh=obh, pc=pc, hh=hh: v.tensor_tensor(
                  obh, pc, Bo[:, hh * 1024:hh * 1024 + 1024], op=ALU.add),
                  reads=kbs + ["Bo"], writes=[f"ob{j}"])
                E("dve", lambda v, obh=obh, j=j, hh=hh: v.tensor_tensor(
                  obh, obh, gf[j][:, hh * 1024:hh * 1024 + 1024], op=ALU.mult),
                  reads=[f"ob{j}", f"gf{j}"], writes=[f"ob{j}"])
                E("dve", lambda v, obh=obh, j=j, hh=hh: v.tensor_tensor(
                  obh, obh, xb[j][:, hh * 1024:hh * 1024 + 1024], op=ALU.add),
                  reads=[f"ob{j}", f"xb{j}"], writes=[f"ob{j}"])
            E("sp", lambda s, tt=tt, j=j: s.dma_start(
              self.out_d[tt * 128:(tt + 1) * 128, :], ob[j]),
              reads=[f"ob{j}"], dma=True)


def build_program(taps=(), phases=4):
    p = P(taps=taps, phases=phases)
    nc = p.build()
    return nc, p


def kernel(L0_attn_x, L0_attn_Wqkv, L0_attn_bqkv, L0_attn_Wo, L0_attn_bo,
           L0_ffn_router_weight, L0_ffn_e_bias, L0_ffn_w_up, L0_ffn_w_down,
           L0_ffn_shared_up, L0_ffn_shared_down):
    nc, _ = build_program()
    common = {
        "Wqkv": np.ascontiguousarray(L0_attn_Wqkv, dtype=np.float32),
        "bqkv": np.ascontiguousarray(L0_attn_bqkv, dtype=np.float32),
        "Wo": np.ascontiguousarray(L0_attn_Wo, dtype=np.float32),
        "bo": np.ascontiguousarray(L0_attn_bo, dtype=np.float32),
        "rw": np.ascontiguousarray(L0_ffn_router_weight, dtype=np.float32),
        "e_bias": np.ascontiguousarray(L0_ffn_e_bias, dtype=np.float32),
        "w_up": np.ascontiguousarray(L0_ffn_w_up, dtype=np.float32),
        "w_down": np.ascontiguousarray(L0_ffn_w_down, dtype=np.float32),
        "sh_up": np.ascontiguousarray(L0_ffn_shared_up, dtype=np.float32),
        "sh_down": np.ascontiguousarray(L0_ffn_shared_down, dtype=np.float32),
    }
    in_maps = [dict(common, x=np.ascontiguousarray(L0_attn_x[b], dtype=np.float32))
               for b in range(8)]
    res = run_bass_kernel_spmd(nc, in_maps, core_ids=list(range(8)))
    return np.stack([np.asarray(r["out"]) for r in res.results])


# version 10 (final)
# speedup vs baseline: 1.2151x; 1.0119x over previous
import math
import numpy as np
import concourse.bass as bass
import concourse.mybir as mybir
from concourse.bass_utils import run_bass_kernel_spmd

F32 = mybir.dt.float32
BF16 = mybir.dt.bfloat16
AF = mybir.ActivationFunctionType
ALU = mybir.AluOpType
AX = mybir.AxisListType

S = 2048
DM = 2048
NH, HS = 16, 128
NE = 16
IM = 1024
NT = 16
RSQ = 1.0 / math.sqrt(HS)

ENGS = ("pe", "act", "dve", "pool", "sp")
BLKNAME = {"pe": "tensor", "act": "scalar", "dve": "vector", "pool": "gpsimd", "sp": "sync"}
NDMASEM = 8


class P:
    def __init__(self, taps=(), phases=4):
        self.taps = set(taps)
        self.phases = phases
        self.nc = bass.Bass("TRN2", target_bir_lowering=False)
        self.ops = {e: [] for e in ENGS}
        self.res = {}
        self.cnt = {}
        self.waited = {e: {} for e in ENGS}
        self.sems = {}
        self.sem_order = []
        for e in ENGS:
            self._newsem("c_" + e)
        for q in ("sp", "pool"):
            for i in range(NDMASEM):
                self._newsem(f"d_{q}{i}")
        self.dma_i = {"sp": 0, "pool": 0}
        self.tap_tensors = {}

    def _newsem(self, name):
        self.cnt[name] = 0
        self.sem_order.append(name)

    def emit(self, eng, fn, reads=(), writes=(), dma=False, tag=""):
        waits = {}

        def addw(tok):
            if tok is not None:
                s, v = tok
                if v > waits.get(s, 0):
                    waits[s] = v

        for k in reads:
            r = self.res.get(k)
            if r:
                addw(r[0])
        for k in writes:
            r = self.res.get(k)
            if r:
                addw(r[0])
                for t in r[1]:
                    addw(t)
        w = self.waited[eng]
        waits = {s: v for s, v in waits.items() if v > w.get(s, 0)}
        for s, v in waits.items():
            w[s] = v
        if dma:
            i = self.dma_i[eng]
            self.dma_i[eng] += 1
            sem = f"d_{eng}{i % NDMASEM}"
            step = 16
            if self.cnt[sem] > self.waited[eng].get(sem, 0):
                waits[sem] = self.cnt[sem]
                self.waited[eng][sem] = self.cnt[sem]
        else:
            sem = "c_" + eng
            step = 1
        self.cnt[sem] += step
        tok = (sem, self.cnt[sem])
        for k in writes:
            self.res[k] = [tok, []]
        for k in reads:
            r = self.res.setdefault(k, [None, []])
            r[1].append(tok)
        self.ops[eng].append((waits, fn, sem, step, tag))
        return tok

    def inherit(self, newkeys, oldkeys):
        toks = []
        for k in oldkeys:
            r = self.res.get(k)
            if r:
                if r[0]:
                    toks.append(r[0])
                toks.extend(r[1])
        for k in newkeys:
            r = self.res.setdefault(k, [None, []])
            r[1].extend(toks)

    def finish(self):
        for q in ("sp", "pool"):
            waits = {}
            for i in range(NDMASEM):
                s = f"d_{q}{i}"
                if self.cnt[s] > self.waited[q].get(s, 0):
                    waits[s] = self.cnt[s]
            if waits:
                self.ops[q].append((waits, None, None, 0, "final"))

    def realize(self):
        nc = self.nc
        from contextlib import ExitStack
        stack = ExitStack()
        for name in self.sem_order:
            self.sems[name] = stack.enter_context(nc.semaphore(name))
        blk = stack.enter_context(nc.Block())
        for eng in ENGS:
            ops = self.ops[eng]
            if not ops:
                continue

            def runner(handle, ops=ops):
                for waits, fn, sem, step, tag in ops:
                    for s, v in waits.items():
                        handle.wait_ge(self.sems[s], v)
                    if fn is not None:
                        fn(handle).then_inc(self.sems[sem], step)
            getattr(blk, BLKNAME[eng])(runner)
        stack.close()

    def tap(self, name, shape, dtype):
        if name in self.taps and name not in self.tap_tensors:
            self.tap_tensors[name] = self.nc.dram_tensor(
                "tap_" + name, shape, dtype, kind="ExternalOutput").ap()
        return self.tap_tensors.get(name)

    def _enter(self, cm):
        from contextlib import ExitStack
        if not hasattr(self, "_alloc_stack"):
            self._alloc_stack = ExitStack()
        return self._alloc_stack.enter_context(cm)

    def wk(self, off, shape, dt):
        n_el = 1
        for d in shape[1:]:
            n_el *= d
        nbytes = n_el * (4 if dt == F32 else 2)
        a = self.wk_sb.ap()[:, off // 2:(off + nbytes) // 2]
        if dt == F32:
            a = a.bitcast(F32)
        if len(shape) == 3:
            a = a.rearrange("p (a b) -> p a b", b=shape[2])
        elif len(shape) == 4:
            a = a.rearrange("p (a b c) -> p a b c", b=shape[2], c=shape[3])
        return a

    def build(self):
        nc = self.nc
        self.x_d = nc.dram_tensor("x", [S, DM], F32, kind="ExternalInput").ap()
        self.wqkv_d = nc.dram_tensor("Wqkv", [DM, 3 * DM], F32, kind="ExternalInput").ap()
        self.bqkv_d = nc.dram_tensor("bqkv", [3 * DM], F32, kind="ExternalInput").ap()
        self.wo_d = nc.dram_tensor("Wo", [DM, DM], F32, kind="ExternalInput").ap()
        self.bo_d = nc.dram_tensor("bo", [DM], F32, kind="ExternalInput").ap()
        self.rw_d = nc.dram_tensor("rw", [NE, DM], F32, kind="ExternalInput").ap()
        self.ebias_d = nc.dram_tensor("e_bias", [NE], F32, kind="ExternalInput").ap()
        self.wup_d = nc.dram_tensor("w_up", [NE, DM, IM], F32, kind="ExternalInput").ap()
        self.wdn_d = nc.dram_tensor("w_down", [NE, IM, DM], F32, kind="ExternalInput").ap()
        self.shup_d = nc.dram_tensor("sh_up", [DM, 4 * IM], F32, kind="ExternalInput").ap()
        self.shdn_d = nc.dram_tensor("sh_down", [4 * IM, DM], F32, kind="ExternalInput").ap()
        self.out_d = nc.dram_tensor("out", [S, DM], F32, kind="ExternalOutput").ap()
        self.g_d = nc.dram_tensor("g_scr", [S, DM], BF16, kind="Internal").ap()

        def sb(n, shp, dt):
            setattr(self, n, self._enter(nc.sbuf_tensor(n, shp, dt)))
        sb("xn_sb", [128, NT, DM], BF16)
        sb("big_sb", [128, NT, DM], BF16)
        sb("wk_sb", [128, 38912], BF16)
        sb("ident_f", [128, 128], F32)
        sb("ident_b", [128, 128], BF16)
        sb("mask_add", [128, 128], F32)
        sb("rwT", [128, NT, NE], F32)
        sb("bq_cols", [128, 48], F32)
        sb("bq_sc", [128, NH], F32)
        sb("ebias_bc", [128, NE], F32)
        sb("eidx", [128, NE], F32)
        sb("gidx", [128, 4], F32)
        sb("combine", [128, NT, NE], F32)
        sb("stats", [128, 32], F32)
        self.logits_ap = self.wk(73728, [128, NT, NE], F32)
        def bigc(off, shape):
            a = self.big_sb.ap().rearrange("p a b -> p (a b)")[:, off // 2:
                off // 2 + shape[-1] * (2 if len(shape) == 3 else 1) * 2]
            a = a.bitcast(F32)
            if len(shape) == 3:
                a = a.rearrange("p (a b) -> p a b", b=shape[2])
            return a
        self.cstage = bigc(0, [128, 128])[0:16, :]
        self.rst = bigc(512, [128, 2, 128])[0:16, :, :]
        self.bstage = bigc(2560, [128, 128])[0:48, :]
        self.psA = self._enter(nc.psum_tensor("psA", [128, 512], F32))
        self.psB = self._enter(nc.psum_tensor("psB", [128, 512], F32))
        self.psC = self._enter(nc.psum_tensor("psC", [128, 2048], F32))
        self.psD = self._enter(nc.psum_tensor("psD", [128, 1024], BF16))
        self.psE = self._enter(nc.psum_tensor("psE", [128, 128], F32))

        self.setup()
        self.p0()
        if self.phases >= 2:
            self.p1()
        if self.phases >= 3:
            self.p2()
        if self.phases >= 4:
            self.p3()
        self.finish()
        self.realize()
        return nc

    def setup(self):
        E = self.emit
        E("pool", lambda g: g.memset(self.ident_f[:], 0.0), writes=["idf"])
        E("pool", lambda g: g.affine_select(out=self.ident_f[:], in_=self.ident_f[:],
          compare_op=ALU.not_equal, fill=1.0, base=0, pattern=[[-1, 128]],
          channel_multiplier=1), reads=["idf"], writes=["idf"])
        E("pool", lambda g: g.memset(self.ident_b[:], 0.0), writes=["idb"])
        E("pool", lambda g: g.affine_select(out=self.ident_b[:], in_=self.ident_b[:],
          compare_op=ALU.not_equal, fill=1.0, base=0, pattern=[[-1, 128]],
          channel_multiplier=1), reads=["idb"], writes=["idb"])
        E("pool", lambda g: g.memset(self.mask_add[:], 0.0), writes=["mask"])
        E("pool", lambda g: g.affine_select(out=self.mask_add[:], in_=self.mask_add[:],
          compare_op=ALU.is_ge, fill=-30000.0, base=0, pattern=[[-1, 128]],
          channel_multiplier=1), reads=["mask"], writes=["mask"])
        E("pool", lambda g: g.iota(self.eidx[:], pattern=[[1, NE]], base=-99,
          channel_multiplier=0, allow_small_or_imprecise_dtypes=True), writes=["eidx"])
        E("pool", lambda g: g.iota(self.gidx[:], pattern=[[1, 4]], base=-99,
          channel_multiplier=0, allow_small_or_imprecise_dtypes=True), writes=["gidx"])
        E("sp", lambda s: s.dma_start(self.ebias_bc[:],
          self.ebias_d.rearrange("(o n) -> o n", o=1).partition_broadcast(128)),
          writes=["ebias"], dma=True)
        E("sp", lambda s: s.dma_start(self.bstage[:],
          self.bqkv_d.rearrange("(a b) -> a b", b=128)), writes=["bstage"], dma=True)
        E("pe", lambda t: t.transpose(self.psA[:, 0:48], self.bstage[:],
          self.ident_f[0:48, 0:48]), reads=["bstage", "idf"], writes=["psA"])
        E("dve", lambda v: v.tensor_copy(self.bq_cols[:], self.psA[:, 0:48]),
          reads=["psA"], writes=["bqc"])
        E("dve", lambda v: v.tensor_scalar(self.bq_sc[:], self.bq_cols[:, 0:16],
          RSQ, None, op0=ALU.mult), reads=["bqc"], writes=["bqsc"])
        for ht in range(NT):
            j = ht % 2
            E("sp", lambda s, ht=ht, j=j: s.dma_start(self.rst[:, j, :],
              self.rw_d[:, ht * 128:(ht + 1) * 128]), writes=[f"rst{j}"], dma=True)
            E("pe", lambda t, j=j: t.transpose(self.psB[:, 0:16],
              self.rst[:, j, :], self.ident_f[0:16, 0:16]),
              reads=[f"rst{j}", "idf"], writes=["psB"])
            E("dve", lambda v, ht=ht: v.tensor_copy(self.rwT.ap()[:, ht, :],
              self.psB[:, 0:16]), reads=["psB"], writes=["rwT"])

    def p0(self):
        import os
        sub = int(os.environ.get("P0SUB", "9"))
        E = self.emit
        xbuf = [self.wk(0, [128, DM], F32), self.wk(8192, [128, DM], F32)]
        xnb = [self.wk(16384, [128, DM], F32), self.wk(24576, [128, DM], F32)]
        dump = self.wk(32768, [128, DM], F32)
        dump2 = self.wk(43008, [128, DM], F32)
        xt32 = [self.wk(40960, [128, 256], F32), self.wk(41984, [128, 256], F32)]
        st = self.stats.ap()
        tapxn = self.tap("xn", [S, DM], F32)

        for tp in range(8):
            for j in range(2):
                tt = tp * 2 + j
                xb, xn = xbuf[j], xnb[j]
                kx, kxn = f"xb{j}", f"xn{j}"
                E("sp", lambda s, tt=tt, xb=xb: s.dma_start(
                    xb, self.x_d[tt * 128:(tt + 1) * 128, :]),
                    writes=[kx], dma=True, tag=f"ldx{tt}")
                o = j * 5
                dmp = [dump, dump2][j]
                kd = f"dump{j}"
                E("dve", lambda v, xb=xb, o=o: v.reduce_sum(st[:, o:o + 1], xb,
                  axis=AX.X, negate=True), reads=[kx], writes=[f"st{o}"])
                E("act", lambda a, xb=xb, o=o, dmp=dmp: a.activation(dmp, xb,
                  AF.Square, accum_out=st[:, o + 1:o + 2]),
                  reads=[kx], writes=[kd, f"st{o+1}"])
                E("dve", lambda v, o=o: v.tensor_scalar(st[:, o:o + 1], st[:, o:o + 1],
                  1.0 / DM, None, op0=ALU.mult), reads=[f"st{o}"], writes=[f"st{o}"])
                E("dve", lambda v, o=o: v.tensor_scalar(st[:, o + 1:o + 2], st[:, o + 1:o + 2],
                  1.0 / DM, None, op0=ALU.mult), reads=[f"st{o+1}"], writes=[f"st{o+1}"])
                E("dve", lambda v, o=o: v.tensor_tensor(st[:, o + 2:o + 3], st[:, o:o + 1],
                  st[:, o:o + 1], op=ALU.mult), reads=[f"st{o}"], writes=[f"st{o+2}"])
                E("dve", lambda v, o=o: v.tensor_tensor(st[:, o + 2:o + 3], st[:, o + 1:o + 2],
                  st[:, o + 2:o + 3], op=ALU.subtract), reads=[f"st{o+1}", f"st{o+2}"], writes=[f"st{o+2}"])
                E("dve", lambda v, o=o: v.tensor_scalar(st[:, o + 2:o + 3], st[:, o + 2:o + 3],
                  1e-5, None, op0=ALU.add), reads=[f"st{o+2}"], writes=[f"st{o+2}"])
                E("act", lambda a, o=o: a.activation(st[:, o + 3:o + 4], st[:, o + 2:o + 3], AF.Ln),
                  reads=[f"st{o+2}"], writes=[f"st{o+3}"])
                E("act", lambda a, o=o: a.activation(st[:, o + 3:o + 4], st[:, o + 3:o + 4], AF.Exp,
                  scale=-0.5), reads=[f"st{o+3}"], writes=[f"st{o+3}"])
                E("dve", lambda v, o=o: v.tensor_tensor(st[:, o + 4:o + 5], st[:, o:o + 1],
                  st[:, o + 3:o + 4], op=ALU.mult), reads=[f"st{o}", f"st{o+3}"], writes=[f"st{o+4}"])
                E("act", lambda a, xb=xb, xn=xn, o=o: a.activation(xn, xb, AF.Identity,
                  bias=st[:, o + 4:o + 5], scale=st[:, o + 3:o + 4]),
                  reads=[kx, f"st{o+3}", f"st{o+4}"], writes=[kxn])
                if tapxn is not None:
                    E("sp", lambda s, tt=tt, xn=xn: s.dma_start(
                        tapxn[tt * 128:(tt + 1) * 128, :], xn), reads=[kxn], dma=True)
            if sub < 2:
                continue
            for ht in range(NT):
                bank = ht % 4
                pc = self.psC[:, bank * 512:bank * 512 + 512]
                kb = f"psC{bank}"
                E("pe", lambda t, ht=ht, pc=pc: t.transpose(
                    pc[:, 0:128], xnb[0][:, ht * 128:(ht + 1) * 128], self.ident_f[:]),
                    reads=["xn0", "idf"], writes=[kb])
                E("pe", lambda t, ht=ht, pc=pc: t.transpose(
                    pc[:, 128:256], xnb[1][:, ht * 128:(ht + 1) * 128], self.ident_f[:]),
                    reads=["xn1", "idf"], writes=[kb])
                E("dve", lambda v, ht=ht, tp=tp, pc=pc: v.tensor_copy(
                    self.xn_sb.ap()[:, ht, tp * 256:tp * 256 + 256], pc[:, 0:256]),
                    reads=[kb], writes=[f"xn:{ht}:{tp // 4}"])
                j2 = ht % 2
                if sub >= 3:
                    E("act", lambda a, pc=pc, j2=j2: a.activation(xt32[j2], pc[:, 0:256], AF.Identity),
                        reads=[kb, f"xn:{ht}:{tp // 4}"], writes=[f"xt32{j2}"])
                if sub >= 4:
                    E("pe", lambda t, ht=ht, j2=j2: t.matmul(
                        self.psA[0:16, 0:256], self.rwT.ap()[:, ht, :], xt32[j2],
                        start=(ht == 0), stop=(ht == NT - 1)),
                        reads=[f"xt32{j2}", "rwT"], writes=["psA"])
            if sub < 5:
                continue
            for j in range(2):
                E("dve", lambda v, j=j: v.tensor_copy(self.cstage[:, 0:128],
                  self.psA[0:16, j * 128:j * 128 + 128]),
                  reads=["psA"], writes=["cst"])
                E("pe", lambda t: t.transpose(self.psB[:, 0:16],
                  self.cstage[:, 0:128], self.ident_f[0:16, 0:16]),
                  reads=["cst", "idf"], writes=["psB"])
                E("dve", lambda v, tp=tp, j=j: v.tensor_copy(
                  self.logits_ap[:, tp * 2 + j, :], self.psB[:, 0:16]),
                  reads=["psB"], writes=["logits"])
        if sub < 6:
            return
        L = self.logits_ap
        r0 = self.wk(65536, [128, NT, NE], F32)
        s_ = self.wk(66560, [128, NT, NE], F32)
        mk = self.wk(67584, [128, NT, NE], F32)
        rm1 = self.wk(68608, [128, NT, NE], F32)
        rm2 = self.wk(69632, [128, NT, NE], F32)
        gs = self.wk(70656, [128, NT, 4], F32)
        gt = self.wk(70912, [128, NT, 4], F32)
        g1m = self.wk(71168, [128, NT, 4], F32)
        g2m = self.wk(71424, [128, NT, 4], F32)
        st2 = self.wk(71680, [128, 12, NT], F32)

        def D(fn, reads=(), writes=()):
            E("dve", fn, reads=list(reads) + ["sel"], writes=list(writes) + ["sel"])

        E("act", lambda a: a.activation(r0.rearrange("p a b -> p (a b)"),
          L.rearrange("p a b -> p (a b)"), AF.Sigmoid),
          reads=["logits", "sel"], writes=["r0sel", "sel"])
        eb = self.ebias_bc.ap().unsqueeze(1).broadcast_to([128, NT, NE])
        D(lambda v: v.tensor_tensor(s_, r0, eb, op=ALU.add),
          reads=["r0sel", "ebias"])
        s4 = s_.rearrange("p t (g e) -> p t g e", e=4)
        pairs = [(0, 1), (0, 2), (0, 3), (1, 2), (1, 3), (2, 3)]
        for pi, (i, j) in enumerate(pairs):
            dst = gs if pi == 0 else gt
            D(lambda v, i=i, j=j, dst=dst: v.tensor_tensor(
              dst, s4[:, :, :, i], s4[:, :, :, j], op=ALU.add))
            if pi:
                D(lambda v: v.tensor_tensor(gs, gs, gt, op=ALU.max))
        gidxb = self.gidx.ap().unsqueeze(1).broadcast_to([128, NT, 4])
        eidxb = self.eidx.ap().unsqueeze(1).broadcast_to([128, NT, NE])

        def pick(vec, idxb, mrow, irow, dstm, width, kidx):
            mb = st2[:, mrow, :].unsqueeze(2).broadcast_to([128, NT, width])
            ib = st2[:, irow, :].unsqueeze(2).broadcast_to([128, NT, width])
            D(lambda v: v.tensor_reduce(st2[:, mrow, :], vec, axis=AX.X,
              op=ALU.max))
            D(lambda v: v.tensor_tensor(dstm, vec, mb, op=ALU.is_ge))
            D(lambda v: v.tensor_tensor(dstm, dstm, idxb, op=ALU.mult),
              reads=[kidx])
            D(lambda v: v.tensor_scalar(dstm, dstm, 99.0, None, op0=ALU.add))
            D(lambda v: v.tensor_reduce(st2[:, irow, :], dstm, axis=AX.X,
              op=ALU.min))
            D(lambda v: v.tensor_tensor(dstm, dstm, ib, op=ALU.is_equal))

        pick(gs, gidxb, 0, 1, g1m, 4, "gidx")
        D(lambda v: v.tensor_scalar(gt, g1m, -1e30, None, op0=ALU.mult))
        D(lambda v: v.tensor_tensor(gt, gs, gt, op=ALU.add))
        pick(gt, gidxb, 2, 3, g2m, 4, "gidx")
        D(lambda v: v.tensor_tensor(g1m, g1m, g2m, op=ALU.add))
        mk4 = mk.rearrange("p t (g e) -> p t g e", e=4)
        for jj in range(4):
            D(lambda v, jj=jj: v.tensor_tensor(
              mk4[:, :, :, jj], s4[:, :, :, jj], g1m, op=ALU.mult))
        pick(mk, eidxb, 4, 5, rm1, NE, "eidx")
        D(lambda v: v.tensor_scalar(s_, rm1, -1e30, None, op0=ALU.mult))
        D(lambda v: v.tensor_tensor(s_, mk, s_, op=ALU.add))
        pick(s_, eidxb, 6, 7, rm2, NE, "eidx")
        D(lambda v: v.tensor_tensor(s_, r0, rm1, op=ALU.mult))
        D(lambda v: v.tensor_reduce(st2[:, 8, :], s_, axis=AX.X, op=ALU.add))
        D(lambda v: v.tensor_tensor(s_, r0, rm2, op=ALU.mult))
        D(lambda v: v.tensor_reduce(st2[:, 9, :], s_, axis=AX.X, op=ALU.add))
        D(lambda v: v.tensor_tensor(st2[:, 10, :], st2[:, 8, :],
          st2[:, 9, :], op=ALU.add))
        D(lambda v: v.tensor_scalar(st2[:, 10, :], st2[:, 10, :], 1e-20,
          None, op0=ALU.add))
        D(lambda v: v.reciprocal(st2[:, 11, :], st2[:, 10, :]))
        D(lambda v: v.tensor_tensor(st2[:, 8, :], st2[:, 8, :],
          st2[:, 11, :], op=ALU.mult))
        D(lambda v: v.tensor_tensor(st2[:, 9, :], st2[:, 9, :],
          st2[:, 11, :], op=ALU.mult))
        C = self.combine.ap()
        w1b = st2[:, 8, :].unsqueeze(2).broadcast_to([128, NT, NE])
        w2b = st2[:, 9, :].unsqueeze(2).broadcast_to([128, NT, NE])
        D(lambda v: v.tensor_tensor(C, rm1, w1b, op=ALU.mult))
        D(lambda v: v.tensor_tensor(s_, rm2, w2b, op=ALU.mult))
        D(lambda v: v.tensor_tensor(C, C, s_, op=ALU.add), writes=["comb"])
        taplog = self.tap("logits", [S, NE], F32)
        tapcmb = self.tap("combine", [S, NE], F32)
        if taplog is not None:
            for tt in range(NT):
                E("sp", lambda s, tt=tt: s.dma_start(
                  taplog[tt * 128:(tt + 1) * 128, :], L[:, tt, :]),
                  reads=["logits", "sel"], dma=True)
                E("sp", lambda s, tt=tt: s.dma_start(
                  tapcmb[tt * 128:(tt + 1) * 128, :], C[:, tt, :]),
                  reads=["comb"], dma=True)

    def p1(self):
        E = self.emit
        self.inherit(["Aebuf", "wdnh0", "wdnh1", "wup0a", "wup0b", "wup1a",
                      "wup1b", "wup2a", "wup2b", "tmp0", "tmp1",
                      "scr1a", "scr1b", "gst"],
                     ["xb0", "xb1", "xn0", "xn1", "dump0", "dump1", "xt320",
                      "xt321", "logits", "sel"])
        self.inherit([f"rt{tt}{hh}" for tt in range(8) for hh in range(2)],
                     ["cst", "bstage", "rst0", "rst1"])
        Ae = self.wk(0, [128, 8, IM], BF16)
        wdnh = [self.wk(16384, [128, 8, 1024], BF16),
                self.wk(32768, [128, 8, 1024], BF16)]
        wup = [self.wk(49152, [128, NT, 128], BF16),
               self.wk(53248, [128, NT, 128], BF16),
               self.wk(57344, [128, NT, 128], BF16)]
        scr1 = [self.wk(61440, [128, 512], F32), self.wk(63488, [128, 512], F32)]
        tmp = [self.wk(65536, [128, 1024], F32), self.wk(69632, [128, 1024], F32)]
        gst = self.wk(73728, [128, DM], BF16)
        routed = self.big_sb.ap().rearrange("p a b -> p (a b)").bitcast(F32) \
                     .rearrange("p (tt h) -> p tt h", h=DM)
        comb = self.combine.ap()

        order_e = [NE, NE + 1, NE + 2, NE + 3] + list(range(NE))
        chunks = [(H, e, ic) for H in range(2) for e in order_e
                  for ic in range(8)]
        dhalves = [(H, e, hh) for H in range(2) for e in order_e
                   for hh in range(2)]

        def load_up(c):
            _, e, ic = chunks[c]
            wb = wup[c % 3]
            if e < NE:
                src = self.wup_d[e, :, ic * 128:(ic + 1) * 128]
            else:
                jj = e - NE
                src = self.shup_d[:, jj * IM + ic * 128:jj * IM + (ic + 1) * 128]
            for half in range(2):
                E("pool", lambda g, src=src, wb=wb, half=half: g.dma_start(
                  wb[:, half * 8:half * 8 + 8, :],
                  src.rearrange("(ht p) c -> p ht c", p=128)[:, half * 8:half * 8 + 8, :]),
                  writes=[f"wup{c % 3}{'ab'[half]}"], dma=True)

        def load_dn_piece(d, q):
            _, e, hh = dhalves[d]
            if e < NE:
                src2 = self.wdn_d[e, :, hh * 1024:hh * 1024 + 1024]
            else:
                jj = e - NE
                src2 = self.shdn_d[jj * IM:(jj + 1) * IM, hh * 1024:hh * 1024 + 1024]
            sr = src2.rearrange("(ii p) h -> p ii h", p=128)
            E("pool", lambda g, sr=sr, d=d, q=q: g.dma_start(
              wdnh[d % 2][:, q * 2:q * 2 + 2, :], sr[:, q * 2:q * 2 + 2, :]),
              writes=[f"wdnh{d % 2}"], dma=True)

        for c in range(3):
            load_up(c)
        ci = 3
        for H in range(2):
            t0 = H * 1024
            for oi, e in enumerate(order_e):
                gi = H * 20 + oi
                moe = e < NE
                for ic in range(8):
                    c = gi * 8 + ic
                    wb = wup[c % 3]
                    kwb = [f"wup{c % 3}a", f"wup{c % 3}b"]
                    for tc in range(2):
                        ps = [self.psA, self.psB][tc]
                        kps = ["psA", "psB"][tc]

                        def mmup(t, wb=wb, ps=ps, t0=t0, tc=tc):
                            for ht in range(NT):
                                mm = t.matmul(ps[:], wb[:, ht, :],
                                    self.xn_sb.ap()[:, ht, t0 + tc * 512:t0 + tc * 512 + 512],
                                    start=(ht == 0), stop=(ht == NT - 1))
                            return mm
                        E("pe", mmup, reads=kwb + [f"xn:{h}:{H}" for h in range(NT)],
                          writes=[kps], tag=f"up{e}")
                        s1 = scr1[tc]
                        ks1 = f"scr1{'ab'[tc]}"
                        E("act", lambda a, ps=ps, s1=s1: a.activation(s1, ps[:],
                          AF.Relu), reads=[kps], writes=[ks1])
                        dst = Ae[:, ic, tc * 512:tc * 512 + 512]
                        E("act", lambda a, s1=s1, dst=dst: a.activation(
                          dst, s1, AF.Square), reads=[ks1], writes=["Aebuf"])
                    if ci < len(chunks):
                        load_up(ci)
                        ci += 1
                    load_dn_piece(gi * 2 + ic // 4, ic % 4)
                for hh in range(2):
                    d = gi * 2 + hh
                    kwd = f"wdnh{d % 2}"
                    for tt in range(8):
                        bp = tt % 2
                        pc = self.psC[:, bp * 1024:bp * 1024 + 1024]
                        kbs = [f"psC{bp * 2}", f"psC{bp * 2 + 1}"]

                        def mmdn(t, tt=tt, pc=pc, d=d):
                            for hc in range(2):
                                for ii in range(8):
                                    mm = t.matmul(
                                        pc[:, hc * 512:hc * 512 + 512],
                                        Ae[:, ii, tt * 128:tt * 128 + 128],
                                        wdnh[d % 2][:, ii, hc * 512:hc * 512 + 512],
                                        start=(ii == 0), stop=(ii == 7))
                            return mm
                        E("pe", mmdn, reads=["Aebuf", kwd],
                          writes=kbs, tag=f"dn{e}")
                        dstr = routed[:, tt, hh * 1024:hh * 1024 + 1024]
                        krt = f"rt{tt}{hh}"
                        if moe:
                            tb = tmp[tt % 2]
                            ktb = f"tmp{tt % 2}"
                            E("act", lambda a, pc=pc, tb=tb, e=e, g=H * 8 + tt:
                              a.activation(tb, pc, AF.Identity,
                              scale=comb[:, g, e:e + 1]),
                              reads=kbs + ["comb"], writes=[ktb])
                            E("dve", lambda v, dstr=dstr, tb=tb: v.tensor_tensor(
                              dstr, dstr, tb, op=ALU.add),
                              reads=[ktb, krt], writes=[krt])
                        elif oi == 0:
                            E("dve", lambda v, dstr=dstr, pc=pc: v.tensor_copy(
                              dstr, pc), reads=kbs, writes=[krt])
                        else:
                            E("dve", lambda v, dstr=dstr, pc=pc: v.tensor_tensor(
                              dstr, dstr, pc, op=ALU.add),
                              reads=kbs + [krt], writes=[krt])
            taprt = self.tap("routed", [S, DM], F32)
            tapg = self.tap("g", [S, DM], BF16)
            for tt in range(8):
                gt = H * 8 + tt
                if taprt is not None:
                    E("sp", lambda s, gt=gt, tt=tt: s.dma_start(
                      taprt[gt * 128:(gt + 1) * 128, :], routed[:, tt, :]),
                      reads=[f"rt{tt}0", f"rt{tt}1"], dma=True)
                E("act", lambda a, tt=tt: a.activation(gst, routed[:, tt, :],
                  AF.Silu), reads=[f"rt{tt}0", f"rt{tt}1", "gst"], writes=["gst"])
                E("sp", lambda s, gt=gt: s.dma_start(
                  self.g_d[gt * 128:(gt + 1) * 128, :], gst),
                  reads=["gst"], writes=[f"g:{gt}"], dma=True)
                if tapg is not None:
                    E("sp", lambda s, gt=gt: s.dma_start(
                      tapg[gt * 128:(gt + 1) * 128, :], gst),
                      reads=["gst"], dma=True)

    def p2(self):
        E = self.emit
        self.inherit(["qsb0", "qsb1", "ksb0", "ksb1", "vsb0", "vsb1",
                      "expS", "atb", "Asb"]
                     + [f"sl{j}_{c}{h}" for j in range(2) for c in range(3) for h in range(2)],
                     ["Aebuf", "wdnh0", "wdnh1", "wup0a", "wup0b", "wup1a",
                      "wup1b", "wup2a", "wup2b", "tmp0", "tmp1",
                      "scr1a", "scr1b", "gst"])
        q_sb = [self.wk(0, [128, S], BF16), self.wk(4096, [128, S], BF16)]
        k_sb = [self.wk(8192, [128, S], BF16), self.wk(12288, [128, S], BF16)]
        v_sb = [self.wk(16384, [128, NT, HS], BF16),
                self.wk(20480, [128, NT, HS], BF16)]
        slab = [self.wk(24576, [128, 3, NT, 128], BF16),
                self.wk(36864, [128, 3, NT, 128], BF16)]
        expS = self.wk(49152, [128, S], F32)
        at_sb = self.wk(57344, [128, S], BF16)
        A_sb = self.wk(61440, [128, S], BF16)
        st = self.stats.ap()
        xnk = [f"xn:{h}:0" for h in range(NT)] + [f"xn:{h}:1" for h in range(NT)]

        def load_slab(n):
            jb = n % 2
            for c in range(3):
                for half in range(2):
                    E("pool", lambda g, n=n, c=c, half=half, jb=jb: g.dma_start(
                      slab[jb][:, c, half * 8:half * 8 + 8, :],
                      self.wqkv_d[:, c * DM + n * 128:c * DM + (n + 1) * 128]
                      .rearrange("(ht p) c2 -> p ht c2", p=128)[:, half * 8:half * 8 + 8, :]),
                      writes=[f"sl{jb}_{c}{half}"], dma=True, tag=f"slab{n}")

        def qkv_steps(n):
            jb = n % 2
            kslab = [f"sl{jb}_{c}{h}" for c in range(3) for h in range(2)]
            steps = []
            for c, dsti, bias, scale in (
                    (0, 0, self.bq_sc[:, n:n + 1], RSQ),
                    (1, 1, self.bq_cols[:, 16 + n:17 + n], 1.0)):
                dst = (q_sb if c == 0 else k_sb)[jb]
                kdst = ("qsb" if c == 0 else "ksb") + str(jb)
                for tc in range(4):
                    def step(jb=jb, c=c, tc=tc, dst=dst, kdst=kdst, bias=bias,
                             scale=scale, kslab=kslab):
                        ps = [self.psA, self.psB][tc % 2]
                        kps = ["psA", "psB"][tc % 2]

                        def mmqk(t):
                            for ht in range(NT):
                                mm = t.matmul(ps[:], slab[jb][:, c, ht, :],
                                    self.xn_sb.ap()[:, ht, tc * 512:tc * 512 + 512],
                                    start=(ht == 0), stop=(ht == NT - 1))
                            return mm
                        E("pe", mmqk, reads=kslab + xnk, writes=[kps], tag=f"qk{n}")
                        E("act", lambda a: a.activation(
                          dst[:, tc * 512:tc * 512 + 512], ps[:], AF.Identity,
                          bias=bias, scale=scale),
                          reads=[kps, "bqc", "bqsc"], writes=[kdst])
                    steps.append(step)
            for tt in range(NT):
                def step(jb=jb, tt=tt, kslab=kslab):
                    psv = [self.psA, self.psB][tt % 2]
                    kpsv = ["psA", "psB"][tt % 2]

                    def mmv(t):
                        for ht in range(NT):
                            mm = t.matmul(psv[:, 0:128],
                                self.xn_sb.ap()[:, ht, tt * 128:tt * 128 + 128],
                                slab[jb][:, 2, ht, :], start=(ht == 0), stop=(ht == NT - 1))
                        return mm
                    E("pe", mmv, reads=kslab + xnk, writes=[kpsv], tag=f"v{n}")
                    E("dve", lambda v: v.tensor_copy(v_sb[jb][:, tt, :],
                      psv[:, 0:128]), reads=[kpsv], writes=[f"vsb{jb}"])
                steps.append(step)
            return steps

        def attn(n, fillers):
            jb = n % 2
            kq, kk, kv = f"qsb{jb}", f"ksb{jb}", f"vsb{jb}"
            fi = 0
            for tt in range(NT):
                ln = (tt + 1) * 128
                nch = (ln + 511) // 512
                for ch in range(nch):
                    w = min(512, ln - ch * 512)
                    pc = self.psC[:, ch * 512:ch * 512 + w]
                    kb = f"psC{ch}"
                    E("pe", lambda t, tt=tt, w=w, ch=ch, pc=pc: t.matmul(
                      pc, q_sb[jb][:, tt * 128:tt * 128 + 128],
                      k_sb[jb][:, ch * 512:ch * 512 + w], start=True, stop=True),
                      reads=[kq, kk], writes=[kb], tag=f"s{n}")
                    if ch == tt // 4:
                        off = (tt % 4) * 128
                        E("dve", lambda v, pc=pc, off=off: v.tensor_tensor(
                          pc[:, off:off + 128], pc[:, off:off + 128],
                          self.mask_add[:], op=ALU.add),
                          reads=[kb, "mask"], writes=[kb])
                    E("act", lambda a, pc=pc, ch=ch, w=w: a.activation(
                      expS[:, ch * 512:ch * 512 + w], pc, AF.Exp,
                      accum_out=st[:, 25 + ch:26 + ch]),
                      reads=[kb], writes=["expS", f"st{25 + ch}"])
                for ch in range(1, nch):
                    E("dve", lambda v, ch=ch: v.tensor_tensor(st[:, 25:26],
                      st[:, 25:26], st[:, 25 + ch:26 + ch], op=ALU.add),
                      reads=["st25", f"st{25 + ch}"], writes=["st25"])
                E("dve", lambda v: v.reciprocal(st[:, 30:31], st[:, 25:26]),
                  reads=["st25"], writes=["st30"])
                E("dve", lambda v, ln=ln: v.tensor_scalar(A_sb[:, 0:ln],
                  expS[:, 0:ln], st[:, 30:31], None, op0=ALU.mult),
                  reads=["expS", "st30"], writes=["Asb"])
                want = min((tt + 1) * len(fillers) // NT, len(fillers))
                while fi < want:
                    fillers[fi]()
                    fi += 1
                groups = list(range(0, tt + 1, 8))

                def mmtp(t, r0, nb):
                    for b in range(nb):
                        mm = t.transpose(
                            self.psD[:, b * 128:b * 128 + 128],
                            A_sb[:, (r0 + b) * 128:(r0 + b) * 128 + 128],
                            self.ident_b[:])
                    return mm

                def mmpv(t, r0, nb, tt):
                    for b in range(nb):
                        ss = r0 + b
                        mm = t.matmul(self.psE[:], v_sb[jb][:, ss, :],
                            at_sb[:, ss * 128:ss * 128 + 128],
                            start=(ss == 0), stop=(ss == tt))
                    return mm

                def emit_tp(r0):
                    nb = min(8, tt + 1 - r0)
                    E("pe", lambda t, r0=r0, nb=nb: mmtp(t, r0, nb),
                      reads=["Asb", "idb", "psD"], writes=["psD"],
                      tag=f"tp{n}")
                    E("dve", lambda v, r0=r0, nb=nb: v.tensor_copy(
                      at_sb[:, r0 * 128:r0 * 128 + nb * 128],
                      self.psD[:, 0:nb * 128]),
                      reads=["psD"], writes=["atb", "psD"])

                def emit_pv(r0):
                    nb = min(8, tt + 1 - r0)
                    E("pe", lambda t, r0=r0, nb=nb, tt=tt: mmpv(t, r0, nb, tt),
                      reads=["atb", kv], writes=["psE"], tag=f"pv{n}")

                for gi2, r0 in enumerate(groups):
                    emit_tp(r0)
                    if gi2 >= 1:
                        emit_pv(groups[gi2 - 1])
                emit_pv(groups[-1])
                E("act", lambda a, n=n, tt=tt: a.activation(
                  self.big_sb.ap()[:, n, tt * 128:tt * 128 + 128], self.psE[:],
                  AF.Identity, bias=self.bq_cols[:, 32 + n:33 + n]),
                  reads=["psE", "bqc"], writes=[f"O{n}"])
            while fi < len(fillers):
                fillers[fi]()
                fi += 1

        load_slab(0)
        for s_ in qkv_steps(0):
            s_()
        load_slab(1)
        for n in range(NH):
            if n + 2 < NH:
                load_slab(n + 2)
            attn(n, qkv_steps(n + 1) if n + 1 < NH else [])

    def p3(self):
        E = self.emit
        self.inherit(["xb0", "xb1", "gld0", "gld1", "gf0", "gf1", "ob0", "ob1", "Bo"],
                     ["qsb0", "qsb1", "ksb0", "ksb1", "vsb0", "vsb1",
                      "expS", "atb", "Asb"]
                     + [f"sl{j}_{c}{h}" for j in range(2) for c in range(3) for h in range(2)])
        xb = [self.wk(0, [128, DM], F32), self.wk(8192, [128, DM], F32)]
        gld = [self.wk(16384, [128, DM], BF16), self.wk(20480, [128, DM], BF16)]
        gf = [self.wk(24576, [128, DM], F32), self.wk(32768, [128, DM], F32)]
        ob = [self.wk(40960, [128, DM], F32), self.wk(49152, [128, DM], F32)]
        Bo = self.wk(57344, [128, DM], F32)
        self.inherit(["wo0", "wo1", "wo2", "wo3"],
                     [f"xn:{h}:0" for h in range(NT)] + [f"xn:{h}:1" for h in range(NT)])
        for q4 in range(4):
            E("pool", lambda g, q4=q4: g.dma_start(
              self.xn_sb.ap()[:, q4 * 4:q4 * 4 + 4, :],
              self.wo_d.rearrange("(ht p) h2 -> p ht h2", p=128)[:, q4 * 4:q4 * 4 + 4, :]),
              writes=[f"wo{q4}"], dma=True, tag="wo")
        E("sp", lambda s: s.dma_start(Bo,
          self.bo_d.rearrange("(o n) -> o n", o=1).partition_broadcast(128)),
          writes=["Bo"], dma=True)
        kwo = [f"wo{q4}" for q4 in range(4)]
        for tt in range(NT):
            j = tt % 2
            E("sp", lambda s, tt=tt, j=j: s.dma_start(xb[j],
              self.x_d[tt * 128:(tt + 1) * 128, :]), writes=[f"xb{j}"], dma=True)
            E("sp", lambda s, tt=tt, j=j: s.dma_start(gld[j],
              self.g_d[tt * 128:(tt + 1) * 128, :]),
              reads=[f"g:{tt}"], writes=[f"gld{j}"], dma=True)
            E("act", lambda a, j=j: a.activation(gf[j], gld[j], AF.Identity),
              reads=[f"gld{j}"], writes=[f"gf{j}"])
            for hh in range(2):
                pc = self.psC[:, hh * 1024:hh * 1024 + 1024]
                kbs = [f"psC{hh * 2}", f"psC{hh * 2 + 1}"]

                def mmop(t, tt=tt, hh=hh, pc=pc):
                    for hc in range(2):
                        for ht in range(NT):
                            mm = t.matmul(pc[:, hc * 512:hc * 512 + 512],
                                self.big_sb.ap()[:, ht, tt * 128:tt * 128 + 128],
                                self.xn_sb.ap()[:, ht, hh * 1024 + hc * 512:hh * 1024 + hc * 512 + 512],
                                start=(ht == 0), stop=(ht == NT - 1))
                    return mm
                E("pe", mmop, reads=kwo + [f"O{h}" for h in range(NH)],
                  writes=kbs, tag=f"op{tt}")
                obh = ob[j][:, hh * 1024:hh * 1024 + 1024]
                E("dve", lambda v, obh=obh, pc=pc, hh=hh: v.tensor_tensor(
                  obh, pc, Bo[:, hh * 1024:hh * 1024 + 1024], op=ALU.add),
                  reads=kbs + ["Bo"], writes=[f"ob{j}"])
                E("dve", lambda v, obh=obh, j=j, hh=hh: v.tensor_tensor(
                  obh, obh, gf[j][:, hh * 1024:hh * 1024 + 1024], op=ALU.mult),
                  reads=[f"ob{j}", f"gf{j}"], writes=[f"ob{j}"])
                E("dve", lambda v, obh=obh, j=j, hh=hh: v.tensor_tensor(
                  obh, obh, xb[j][:, hh * 1024:hh * 1024 + 1024], op=ALU.add),
                  reads=[f"ob{j}", f"xb{j}"], writes=[f"ob{j}"])
            E("sp", lambda s, tt=tt, j=j: s.dma_start(
              self.out_d[tt * 128:(tt + 1) * 128, :], ob[j]),
              reads=[f"ob{j}"], dma=True)


def build_program(taps=(), phases=4):
    p = P(taps=taps, phases=phases)
    nc = p.build()
    return nc, p


def kernel(L0_attn_x, L0_attn_Wqkv, L0_attn_bqkv, L0_attn_Wo, L0_attn_bo,
           L0_ffn_router_weight, L0_ffn_e_bias, L0_ffn_w_up, L0_ffn_w_down,
           L0_ffn_shared_up, L0_ffn_shared_down):
    nc, _ = build_program()
    common = {
        "Wqkv": np.ascontiguousarray(L0_attn_Wqkv, dtype=np.float32),
        "bqkv": np.ascontiguousarray(L0_attn_bqkv, dtype=np.float32),
        "Wo": np.ascontiguousarray(L0_attn_Wo, dtype=np.float32),
        "bo": np.ascontiguousarray(L0_attn_bo, dtype=np.float32),
        "rw": np.ascontiguousarray(L0_ffn_router_weight, dtype=np.float32),
        "e_bias": np.ascontiguousarray(L0_ffn_e_bias, dtype=np.float32),
        "w_up": np.ascontiguousarray(L0_ffn_w_up, dtype=np.float32),
        "w_down": np.ascontiguousarray(L0_ffn_w_down, dtype=np.float32),
        "sh_up": np.ascontiguousarray(L0_ffn_shared_up, dtype=np.float32),
        "sh_down": np.ascontiguousarray(L0_ffn_shared_down, dtype=np.float32),
    }
    in_maps = [dict(common, x=np.ascontiguousarray(L0_attn_x[b], dtype=np.float32))
               for b in range(8)]
    res = run_bass_kernel_spmd(nc, in_maps, core_ids=list(range(8)))
    return np.stack([np.asarray(r["out"]) for r in res.results])
